# version 08
# baseline (speedup 1.0000x reference)
import numpy as np
import concourse.bass as bass
import concourse.mybir as mybir
from concourse.alu_op_type import AluOpType as OP
from concourse.bass_utils import run_bass_kernel_spmd

F32 = mybir.dt.float32
BF16 = mybir.dt.bfloat16
U8 = mybir.dt.uint8
AF = mybir.ActivationFunctionType
AX = mybir.AxisListType.X

S = 4096
C = 1024
NG = 8
TG = 4
EPS = 1e-5
ENGS = ("tensor", "vector", "scalar", "gpsimd", "sync")


class Prog:

    def __init__(self):
        self.counts = {}
        self.ops = {e: [] for e in ENGS}
        self.lastw = {e: {} for e in ENGS}
        for s_ in ("sD", "sDo", "sDg", "sV", "sS", "sT", "sG", "sC"):
            self.counts[s_] = 0

    def emit(self, eng, build, waits=(), inc=None, n=1, fence=False):
        w = []
        for t in waits:
            if t is None:
                continue
            s, v = t
            if self.lastw[eng].get(s, 0) >= v:
                continue
            self.lastw[eng][s] = v
            w.append((s, v))
        tok = None
        if inc is not None:
            self.counts[inc] += n
            tok = (inc, self.counts[inc])
        self.ops[eng].append((w, build, inc, n, fence))
        return tok

    def dma(self, eng, out, in_, waits=(), inc="sD"):
        return self.emit(eng, lambda e: e.dma_start(out, in_), waits, inc, 16)

    def barrier(self):
        toks = [(s, v) for s, v in self.counts.items() if v > 0]
        for eng in ENGS:
            self.emit(eng, lambda e: e.nop(), waits=toks)

    def build_into(self, block, sems):
        for eng in ENGS:
            ops = self.ops[eng]
            if not ops:
                continue

            def body(e, ops=ops, eng=eng):
                for waits, build, inc, n, fence in ops:
                    for s, v in waits:
                        e.wait_ge(sems[s], v)
                    ins = build(e)
                    if inc is not None:
                        if fence:
                            e.drain().then_inc(sems[inc], n)
                        else:
                            ins.then_inc(sems[inc], n)

            getattr(block, eng)(body)


def T(*toks):
    return [t for t in toks if t is not None]


def build_program(n_sub=6):
    nc = bass.Bass("TRN2", target_bir_lowering=False, detect_race_conditions=False)

    x_in = nc.dram_tensor("x", [S, C], F32, kind="ExternalInput")
    y_out = nc.dram_tensor("y", [S, C], F32, kind="ExternalOutput")
    WD = {}
    for nm, shp in [
        ("L0_attn_W_q", [C, C]), ("L0_attn_W_k", [C, C]), ("L0_attn_W_v", [C, C]),
        ("L0_attn_W_o", [C, C]), ("L0_ffn_W_r", [C, C]), ("L0_ffn_W_k", [C, C]),
        ("L0_ffn_W_v", [C, C]), ("L0_ffn_mu_r", [C]), ("L0_ffn_mu_k", [C]),
        ("L1_attn_W_qkv", [C, 3 * C]), ("L1_attn_b_qkv", [3 * C]),
        ("L1_attn_W_proj", [C, C]), ("L1_attn_b_proj", [C]),
        ("L1_ffn_weight", [C, C]), ("L1_ffn_bias", [C]),
    ]:
        WD[nm] = nc.dram_tensor(nm, shp, F32, kind="ExternalInput")
    xs = [nc.dram_tensor(f"xs{i}", [S, C], F32) for i in range(1, 6)]
    qTd = nc.dram_tensor("qTd", [8, 128, S], BF16)
    ktd = nc.dram_tensor("ktd", [8, 128, S], BF16)
    vtd = nc.dram_tensor("vtd", [8, 128, S // 128, 128], BF16)
    ccio = [(nc.dram_tensor(f"cci{i}", [1, 1], F32), nc.dram_tensor(f"cco{i}", [1, 1], F32))
            for i in range(2)]

    p = Prog()
    E = p.emit

    sb = nc.alloc_sbuf_tensor
    ident_b = sb("ident_b", [128, 128], BF16)
    ident_f = sb("ident_f", [128, 128], F32)
    i0 = sb("i0", [128, 128], F32)
    ones_row = sb("ones_row", [1, 128], F32)
    ones64 = sb("ones64", [128, 64], BF16)
    cols = sb("cols", [128, 64], F32)
    lnjunk = sb("lnjunk", [128, C], F32)
    mu_sb = sb("mu_sb", [128, 16], F32)
    bqkv_col = sb("bqkv_col", [128, 24], F32)
    scal = sb("scal", [128, 16], F32)
    arow = sb("arow", [1, 8], F32)
    PERSIST_BASE = nc.sbuf_base
    SBUF_TOP = nc.sbuf_top

    E("gpsimd", lambda g: g.iota(i0[:, :], [[1, 128]], channel_multiplier=0,
                                 allow_small_or_imprecise_dtypes=True))
    tIo = E("gpsimd", lambda g: g.iota(ident_f[:, :], [[0, 128]], channel_multiplier=1,
                                       allow_small_or_imprecise_dtypes=True), inc="sG")
    tIb = E("vector", lambda v: v.tensor_tensor(ident_b[:, :], i0[:, :], ident_f[:, :],
                                                OP.is_equal), waits=[tIo], inc="sV")
    tIf = E("vector", lambda v: v.tensor_tensor(ident_f[:, :], i0[:, :], ident_f[:, :],
                                                OP.is_equal), waits=[tIb], inc="sV")
    tOnes = E("gpsimd", lambda g: g.memset(ones_row[:, :], 1.0), inc="sG")
    tO64 = E("gpsimd", lambda g: g.memset(ones64[:, :], 1.0), inc="sG")

    def arena(tag, items):
        nc.sbuf_base = PERSIST_BASE
        out = {}
        for nm, shp, dt in items:
            out[nm] = nc.alloc_sbuf_tensor(tag + nm, shp, dt).ap()
        assert nc.sbuf_base <= SBUF_TOP, f"SBUF overflow {tag}: {nc.sbuf_base}>{SBUF_TOP}"
        return out

    def psum(tag, items):
        nc.psum_base = 0
        out = {}
        for nm, shp, dt in items:
            h = nc.alloc_psum_tensor(tag + nm, shp, dt)
            out[nm] = h.ap()
            out[nm + "_h"] = h
        assert nc.psum_base <= 8
        return out

    def ln_tile(xg_tile, x0_out, ci, tok_in, bf_out=None):
        c8 = cols[:, ci:ci + 8]
        t1 = E("vector", lambda v: v.reduce_sum(c8[:, 0:1], xg_tile, axis=AX),
               waits=tok_in, inc="sV")
        t2 = E("vector", lambda v: v.tensor_scalar(c8[:, 1:2], c8[:, 0:1], -1.0 / C, None,
                                                   OP.mult), waits=[t1], inc="sV")
        t3 = E("scalar", lambda s: s.activation(x0_out, xg_tile, AF.Identity, bias=c8[:, 1:2],
                                                scale=1.0), waits=T(t2, *tok_in), inc="sS")
        t4 = E("scalar", lambda s: s.activation(lnjunk[:, :], xg_tile, AF.Square,
                                                bias=c8[:, 1:2], scale=1.0,
                                                accum_out=c8[:, 2:3]), inc="sS")
        t5 = E("vector", lambda v: v.tensor_scalar(c8[:, 3:4], c8[:, 2:3], 1.0 / C, EPS,
                                                   OP.mult, op1=OP.add), waits=[t4], inc="sV")
        t6 = E("scalar", lambda s: s.activation(c8[:, 4:5], c8[:, 3:4], AF.Sqrt),
               waits=[t5], inc="sS")
        t7 = E("vector", lambda v: v.reciprocal(c8[:, 5:6], c8[:, 4:5]), waits=[t6], inc="sV")
        t8 = E("vector", lambda v: v.tensor_scalar(x0_out, x0_out, c8[:, 5:6], None, OP.mult),
               waits=[t7, t3], inc="sV")
        if bf_out is None:
            return t8
        return E("vector", lambda v: v.tensor_copy(bf_out, x0_out), waits=[t8], inc="sV")

    def transposes_to(psT, src_bf, dst_sb, tok_in, extra=()):
        trs = None
        for j in range(8):
            trs = E("tensor", lambda t, j=j: t.transpose(psT[:, j, :],
                                                         src_bf[:, j * 128:(j + 1) * 128],
                                                         ident_b[:, :]),
                    waits=T(tIb, *tok_in, *extra), inc="sT" if j == 7 else None, fence=(j == 7))
        return E("vector", lambda v: v.tensor_copy(dst_sb, psT[:, :, :]), waits=[trs], inc="sV")

    def cross_part_max(col1, out11, ps_m, tok_in):
        tt = E("tensor", lambda t: t.transpose(ps_m[0:1, 0:128], col1, ident_f[:, :]),
               waits=T(tIf, *tok_in), inc="sT", fence=True)
        return E("vector", lambda v: v.reduce_max(out11, ps_m[0:1, 0:128], axis=AX),
                 waits=[tt], inc="sV")

    def bcast_scalar(dst_col, src11, ps_col, tok_in):
        tm = E("tensor", lambda t: t.matmul(ps_col, ones_row[:, :], src11, start=True,
                                            stop=True), waits=T(tOnes, *tok_in), inc="sT",
               fence=True)
        return E("vector", lambda v: v.tensor_copy(dst_col, ps_col), waits=[tm], inc="sV")

    def prep_wT(wdram, dst, wstage, psW, scale, extra=()):
        dW = p.dma("sync", wstage[:, :, :], wdram.ap().rearrange("(r p) c -> p r c", p=128),
                   waits=list(extra), inc="sD")
        tprev = None
        for j in range(8):
            trs = None
            for r in range(8):
                trs = E("tensor", lambda t, j=j, r=r: t.transpose(
                    psW[:, r, :], wstage[:, r, j * 128:(j + 1) * 128], ident_f[:, :]),
                    waits=T(tIf, dW, tprev), inc="sT" if r == 7 else None, fence=(r == 7))
            tprev = E("vector", lambda v, j=j: v.tensor_scalar(dst[:, j, :], psW[:, :, :],
                                                               scale, None, OP.mult),
                      waits=[trs], inc="sV")
        return tprev, dW

    def quant_chain(src_ap, dst_bf, scol, q1, q2, q3, msk, tok_in):
        C20 = float(2 ** 20 + 1)
        MM = float(1.5 * 2 ** 14)
        E("vector", lambda v: v.tensor_scalar(q1, src_ap, scol, None, OP.mult), waits=tok_in)
        E("vector", lambda v: v.tensor_scalar(q2, q1, C20, None, OP.mult))
        E("vector", lambda v: v.tensor_tensor(q3, q2, q1, OP.subtract))
        E("vector", lambda v: v.tensor_tensor(q2, q2, q3, OP.subtract))
        E("vector", lambda v: v.tensor_tensor(q3, q1, q1, OP.mult))
        E("vector", lambda v: v.tensor_single_scalar(msk, q3, float(2.0 ** -12), OP.is_ge))
        E("vector", lambda v: v.tensor_scalar(q3, q1, MM, None, OP.add))
        E("vector", lambda v: v.tensor_scalar(q3, q3, MM, None, OP.subtract))
        return E("vector", lambda v: v.select(dst_bf, msk, q2, q3), inc="sV")

    x_re = lambda d: d.ap().rearrange("(g t p) c -> g p t c", p=128, t=TG)

    def phase_l0ffn(tag, src, dst):
        A = arena(tag, [
            ("wt", [128, 8, 3072], BF16),
            ("wstage", [128, 8, 1024], F32),
            ("xg", [128, 2, TG, 1024], F32),
            ("x0", [128, 2, 1024], F32),
            ("xb", [128, 2, 1024], BF16),
            ("xT", [128, 8, 513], BF16),
            ("dmix", [128, 512], BF16),
            ("xrT", [128, 8, 512], BF16),
            ("xkT", [128, 8, 512], BF16),
            ("kact", [128, 8, 512], BF16),
            ("relu", [128, 512], F32),
            ("rsb", [128, 2, 1024], F32),
            ("tmp", [128, 2, 1024], F32),
        ])
        P = psum(tag, [
            ("t", [128, 8, 128], BF16),
            ("k", [128, 2, 512], F32),
            ("r", [128, 1024], F32),
            ("v", [128, 1024], F32),
        ])
        psW = P["r"].rearrange("p (a b) -> p a b", a=8)
        wt = A["wt"]
        tw0, _ = prep_wT(WD["L0_ffn_W_r"], wt[:, :, 0:1024], A["wstage"], psW, 1.0)
        tw1, _ = prep_wT(WD["L0_ffn_W_k"], wt[:, :, 1024:2048], A["wstage"], psW, 1.0, [tw0])
        tw2, _ = prep_wT(WD["L0_ffn_W_v"], wt[:, :, 2048:3072], A["wstage"], psW, 0.5, [tw1])
        dmu = p.emit("gpsimd", lambda g: g.dma_start(
            mu_sb[:, 0:8], bass.AP(WD["L0_ffn_mu_r"], 0, [[1, 128], [128, 8]]),
            allow_slow_non_contiguous=True), inc="sDg", n=16)
        dmu2 = p.emit("gpsimd", lambda g: g.dma_start(
            mu_sb[:, 8:16], bass.AP(WD["L0_ffn_mu_k"], 0, [[1, 128], [128, 8]]),
            allow_slow_non_contiguous=True), inc="sDg", n=16)
        tb0 = E("vector", lambda v: v.memset(A["xT"][:, :, 0:1], 0.0), inc="sV")
        wtoks = [tw0, tw1, tw2, dmu, dmu2, tb0]

        xg_free = [None, None]
        bound_tok = tb0
        mix_last = None
        v_mm_last = None
        r_mm_last = None
        tsig_prev = dmu
        tm1_hist = [None, None]
        tm1_prev = None
        t9h = [None, None]
        teh = [None, None]
        dO = None
        for g in range(NG):
            xg = A["xg"][:, g % 2]
            dX = p.dma("sync", xg, x_re(src)[g], waits=T(xg_free[g % 2], *wtoks), inc="sD")
            if g > 0:
                bound_tok = E("vector", lambda v: v.tensor_copy(A["xT"][:, :, 0:1],
                                                                A["xT"][:, :, 512:513]),
                              waits=T(mix_last), inc="sV")
            tr_evac = []
            for t in range(TG):
                t9 = ln_tile(xg[:, t, :], A["x0"][:, t % 2, :], 8 * ((TG * g + t) % 8),
                             T(dX, t9h[t % 2], teh[t % 2]), bf_out=A["xb"][:, t % 2, :])
                t9h[t % 2] = t9
                te = transposes_to(P["t"], A["xb"][:, t % 2, :], A["xT"].rearrange(
                    "p c m -> p c m")[:, :, 1 + 128 * t:1 + 128 * (t + 1)],
                    [t9], extra=T(tr_evac[-1] if tr_evac else None, bound_tok, mix_last))
                teh[t % 2] = te
                tr_evac.append(te)
            mix_toks = []
            for cc in range(8):
                td = E("vector", lambda v, cc=cc: v.tensor_tensor(
                    A["dmix"], A["xT"][:, cc, 1:513], A["xT"][:, cc, 0:513 - 1], OP.subtract),
                    waits=T(tr_evac[-1], bound_tok, dmu, dmu2,
                            r_mm_last), inc="sV")
                tr_ = E("vector", lambda v, cc=cc: v.scalar_tensor_tensor(
                    A["xrT"][:, cc, :], A["dmix"], mu_sb[:, cc:cc + 1], A["xT"][:, cc, 0:512],
                    OP.mult, OP.add), waits=[td], inc="sV")
                tk_ = E("vector", lambda v, cc=cc: v.scalar_tensor_tensor(
                    A["xkT"][:, cc, :], A["dmix"], mu_sb[:, 8 + cc:9 + cc], A["xT"][:, cc, 0:512],
                    OP.mult, OP.add), inc="sV")
                mix_toks += [tr_, tk_]
            mix_last = mix_toks[-1]
            kact_toks = []
            for dch in range(8):
                mm = None
                for cc in range(8):
                    mm = E("tensor", lambda t, dch=dch, cc=cc: t.matmul(
                        P["k"][:, dch % 2, :], wt[:, cc, 1024 + dch * 128:1024 + (dch + 1) * 128],
                        A["xkT"][:, cc, :], start=(cc == 0), stop=(cc == 7)),
                        waits=T(mix_last, kact_toks[-2] if len(kact_toks) >= 2 else None,
                                v_mm_last), inc="sT" if cc == 7 else None, fence=(cc == 7))
                t1 = E("vector", lambda v, dch=dch: v.tensor_single_scalar(
                    A["relu"], P["k"][:, dch % 2, :], 0.0, OP.max), waits=[mm], inc="sV")
                t2 = E("vector", lambda v, dch=dch: v.tensor_tensor(
                    A["kact"][:, dch, :], A["relu"], A["relu"], OP.mult), waits=[t1], inc="sV")
                kact_toks.append(t2)
            out_toks = []
            for t in range(TG):
                mmr = None
                for cc in range(8):
                    for h in range(2):
                        last = (cc == 7 and h == 1)
                        mmr = E("tensor", lambda tt, t=t, cc=cc, h=h: tt.matmul(
                            P["r"][:, h * 512:(h + 1) * 512],
                            A["xrT"][:, cc, 128 * t:128 * (t + 1)],
                            wt[:, cc, h * 512:(h + 1) * 512],
                            start=(cc == 0), stop=(cc == 7)),
                            waits=T(tsig_prev), inc="sT" if last else None, fence=last)
                r_mm_last = mmr
                tsig = E("scalar", lambda s, t=t: s.activation(A["rsb"][:, t % 2, :], P["r"][:, :],
                                                               AF.Sigmoid),
                         waits=T(mmr, tm1_hist[t % 2]), inc="sS")
                tsig_prev = tsig
                mmv = None
                for dch in range(8):
                    for h in range(2):
                        last = (dch == 7 and h == 1)
                        mmv = E("tensor", lambda tt, t=t, dch=dch, h=h: tt.matmul(
                            P["v"][:, h * 512:(h + 1) * 512],
                            A["kact"][:, dch, 128 * t:128 * (t + 1)],
                            wt[:, dch, 2048 + h * 512:2048 + (h + 1) * 512],
                            start=(dch == 0), stop=(dch == 7)),
                            waits=T(kact_toks[-1], tm1_prev),
                            inc="sT" if last else None, fence=last)
                v_mm_last = mmv
                tm1 = E("vector", lambda v, t=t: v.tensor_tensor(
                    A["tmp"][:, t % 2, :], A["rsb"][:, t % 2, :], P["v"][:, :], OP.mult),
                    waits=T(mmv, tsig), inc="sV")
                tm1_hist[t % 2] = tm1
                tm1_prev = tm1
                tm2 = E("vector", lambda v, t=t, xg=xg: v.tensor_tensor(
                    xg[:, t, :], xg[:, t, :], A["tmp"][:, t % 2, :], OP.add),
                    waits=[tm1], inc="sV")
                out_toks.append(tm2)
            dO = p.dma("scalar", x_re(dst)[g], xg, waits=out_toks, inc="sDo")
            xg_free[g % 2] = dO
        return dO

    def phase_l0attn(tag, src, dst):
        A = arena(tag, [
            ("wt", [128, 8, 4096], BF16),
            ("wstage", [128, 8, 1024], F32),
            ("xg", [128, 2, TG, 1024], F32),
            ("x0", [128, 2, 1024], F32),
            ("xb", [128, 2, 1024], BF16),
            ("xT", [128, 8, 512], BF16),
            ("ksb", [128, TG, 1024], BF16),
            ("vsb", [128, TG, 1024], BF16),
            ("kvb", [128, 8, 128], BF16),
            ("aT", [128, 8, 512], BF16),
            ("qst", [128, 2, 512], BF16),
            ("qg", [128, 8, 512], BF16),
        ])
        P = psum(tag, [
            ("t", [128, 8, 128], BF16),
            ("q", [128, 2, 512], F32),
            ("k", [128, 1024], F32),
            ("v", [128, 1024], F32),
            ("kv", [128, 8, 64], F32),
        ])
        wt = A["wt"]
        tw0, _ = prep_wT(WD["L0_attn_W_q"], wt[:, :, 0:1024], A["wstage"], P["k"].rearrange(
            "p (a b) -> p a b", a=8), 0.125)
        tw1, _ = prep_wT(WD["L0_attn_W_k"], wt[:, :, 1024:2048], A["wstage"],
                         P["k"].rearrange("p (a b) -> p a b", a=8), 1.0, [tw0])
        tw2, _ = prep_wT(WD["L0_attn_W_v"], wt[:, :, 2048:3072], A["wstage"],
                         P["k"].rearrange("p (a b) -> p a b", a=8), 1.0, [tw1])
        tw3, _ = prep_wT(WD["L0_attn_W_o"], wt[:, :, 3072:4096], A["wstage"],
                         P["k"].rearrange("p (a b) -> p a b", a=8), 1.0, [tw2])
        wtoks = [tw0, tw1, tw2, tw3]
        xg_free = [None, None]
        kv_last = None
        q_dma = [None, None]
        qev_hist = [None, None]
        ksb_free = None
        t9h = [None, None]
        teh = [None, None]
        for g in range(NG):
            xg = A["xg"][:, g % 2]
            dX = p.dma("sync", xg, x_re(src)[g], waits=T(xg_free[g % 2], *wtoks), inc="sD")
            tr_evac = []
            for t in range(TG):
                t9 = ln_tile(xg[:, t, :], A["x0"][:, t % 2, :], 8 * ((TG * g + t) % 8),
                             T(dX, t9h[t % 2], teh[t % 2]), bf_out=A["xb"][:, t % 2, :])
                t9h[t % 2] = t9
                te = transposes_to(P["t"], A["xb"][:, t % 2, :],
                                   A["xT"][:, :, 128 * t:128 * (t + 1)],
                                   [t9], extra=T(tr_evac[-1] if tr_evac else None, *wtoks))
                teh[t % 2] = te
                tr_evac.append(te)
            q_toks = []
            for pr in range(8):
                mm = None
                for cc in range(8):
                    mm = E("tensor", lambda t, pr=pr, cc=cc: t.matmul(
                        P["q"][:, pr % 2, :], wt[:, cc, pr * 128:(pr + 1) * 128],
                        A["xT"][:, cc, :], start=(cc == 0), stop=(cc == 7)),
                        waits=T(tr_evac[-1], qev_hist[pr % 2]),
                        inc="sT" if cc == 7 else None, fence=(cc == 7))
                ev = E("vector", lambda v, pr=pr: v.tensor_copy(A["qst"][:, pr % 2, :],
                                                                P["q"][:, pr % 2, :]),
                       waits=T(mm, q_dma[pr % 2]), inc="sV")
                dq = p.dma("scalar", bass.AP(qTd, pr * 128 * S + 512 * g, [[S, 128], [1, 512]]),
                           A["qst"][:, pr % 2, :], waits=[ev], inc="sDo")
                q_dma[pr % 2] = dq
                qev_hist[pr % 2] = ev
                q_toks.append(ev)
            kv_sb_toks = []
            for t in range(TG):
                mmk = None
                for cc in range(8):
                    for h in range(2):
                        last = (cc == 7 and h == 1)
                        mmk = E("tensor", lambda tt, t=t, cc=cc, h=h: tt.matmul(
                            P["k"][:, h * 512:(h + 1) * 512],
                            A["xT"][:, cc, 128 * t:128 * (t + 1)],
                            wt[:, cc, 1024 + h * 512:1024 + (h + 1) * 512],
                            start=(cc == 0), stop=(cc == 7)),
                            waits=T(ksb_free), inc="sT" if last else None, fence=last)
                tk = E("vector", lambda v, t=t: v.tensor_copy(A["ksb"][:, t, :], P["k"][:, :]),
                       waits=[mmk], inc="sV")
                mmv = None
                for cc in range(8):
                    for h in range(2):
                        last = (cc == 7 and h == 1)
                        mmv = E("tensor", lambda tt, t=t, cc=cc, h=h: tt.matmul(
                            P["v"][:, h * 512:(h + 1) * 512],
                            A["xT"][:, cc, 128 * t:128 * (t + 1)],
                            wt[:, cc, 2048 + h * 512:2048 + (h + 1) * 512],
                            start=(cc == 0), stop=(cc == 7)),
                            inc="sT" if last else None, fence=last)
                tv = E("vector", lambda v, t=t: v.tensor_copy(A["vsb"][:, t, :], P["v"][:, :]),
                       waits=[mmv], inc="sV")
                kv_sb_toks += [tk, tv]
            for t in range(TG):
                for pr in range(8):
                    first = (g == 0 and t == 0 and pr == 0)
                    last = (g == NG - 1 and t == TG - 1 and pr == 7)
                    glast = (t == TG - 1 and pr == 7)
                    E("tensor", lambda tt, t=t, pr=pr, first=first, last=last: tt.matmul(
                        P["kv"][0:64, pr, :], A["ksb"][:, t, 128 * pr:128 * pr + 64],
                        A["vsb"][:, t, 128 * pr:128 * pr + 64],
                        start=first, stop=last, skip_group_check=True,
                        tile_position=(0, 0)), waits=T(kv_sb_toks[-1]), inc=None)
                    kv_last = E("tensor", lambda tt, t=t, pr=pr, first=first, last=last: tt.matmul(
                        P["kv"][64:128, pr, :], A["ksb"][:, t, 128 * pr + 64:128 * (pr + 1)],
                        A["vsb"][:, t, 128 * pr + 64:128 * (pr + 1)],
                        start=first, stop=last, skip_group_check=True,
                        tile_position=(0, 64)), inc="sT" if glast else None, fence=last)
            ksb_free = kv_last
            xg_free[g % 2] = tr_evac[-1]
        tz = E("vector", lambda v: v.memset(A["kvb"][:, :, :], 0.0), inc="sV")
        kv_ev = tz
        for pr in range(8):
            E("vector", lambda v, pr=pr: v.tensor_copy(A["kvb"][0:64, pr, 0:64],
                                                       P["kv"][0:64, pr, :]),
              waits=T(kv_last, tz))
            kv_ev = E("vector", lambda v, pr=pr: v.tensor_copy(A["kvb"][64:128, pr, 64:128],
                                                               P["kv"][64:128, pr, :]), inc="sV")
        psA2 = P["k"]
        psY = P["v"]
        qg_dma_free = None
        aev_hist = [None, None]
        to_prev = None
        dO = None
        for g in range(NG):
            xg = A["xg"][:, g % 2]
            dX = p.dma("sync", xg, x_re(src)[g], waits=T(xg_free[g % 2], kv_ev), inc="sD")
            dq = p.dma("sync", A["qg"][:, :, :],
                       bass.AP(qTd, 512 * g, [[S, 128], [128 * S, 8], [1, 512]]),
                       waits=T(qg_dma_free, kv_ev, q_dma[0], q_dma[1]), inc="sD")
            a_toks = []
            for pr in range(8):
                b_ = pr % 2
                ma = E("tensor", lambda t, pr=pr, b_=b_: t.matmul(
                    psA2[0:64, b_ * 512:(b_ + 1) * 512], A["kvb"][:, pr, 0:64],
                    A["qg"][:, pr, :], start=True, stop=True, tile_position=(0, 0)),
                    waits=T(kv_ev, dq, aev_hist[b_]), inc=None)
                mb = E("tensor", lambda t, pr=pr, b_=b_: t.matmul(
                    psA2[64:128, b_ * 512:(b_ + 1) * 512], A["kvb"][:, pr, 64:128],
                    A["qg"][:, pr, :], start=True, stop=True, tile_position=(0, 64),
                    skip_group_check=True), inc="sT", fence=True)
                te = E("vector", lambda v, pr=pr, b_=b_: v.tensor_copy(
                    A["aT"][:, pr, :], psA2[:, b_ * 512:(b_ + 1) * 512]),
                    waits=[mb], inc="sV")
                aev_hist[b_] = te
                a_toks.append(te)
            qg_dma_free = a_toks[-1]
            out_toks = []
            for t in range(TG):
                mm = None
                for pr in range(8):
                    for h in range(2):
                        last = (pr == 7 and h == 1)
                        mm = E("tensor", lambda tt, t=t, pr=pr, h=h: tt.matmul(
                            psY[:, h * 512:(h + 1) * 512], A["aT"][:, pr, 128 * t:128 * (t + 1)],
                            wt[:, pr, 3072 + h * 512:3072 + (h + 1) * 512],
                            start=(pr == 0), stop=(pr == 7)),
                            waits=T(a_toks[-1], to_prev), inc="sT" if last else None,
                            fence=last)
                to = E("vector", lambda v, t=t, xg=xg: v.tensor_tensor(
                    xg[:, t, :], xg[:, t, :], psY[:, :], OP.add),
                    waits=T(mm, dX), inc="sV")
                to_prev = to
                out_toks.append(to)
            dO = p.dma("scalar", x_re(dst)[g], xg, waits=out_toks, inc="sDo")
            xg_free[g % 2] = dO
        return dO

    def phase_l1ffn(tag, src, dst, cci, cco):
        A = arena(tag, [
            ("wq", [128, 8, 1024], BF16),
            ("wstage", [128, 8, 1024], F32),
            ("xg", [128, 2, TG, 1024], F32),
            ("xf", [128, 1024], F32),
            ("q1", [128, 1024], F32),
            ("q2", [128, 1024], F32),
            ("q3", [128, 1024], F32),
            ("msk", [128, 1024], U8),
            ("xb", [128, 1024], BF16),
            ("xT", [128, 8, 128], BF16),
            ("bias", [128, 1024], F32),
        ])
        P = psum(tag, [
            ("t", [128, 8, 128], BF16),
            ("w", [128, 8, 128], F32),
            ("y", [128, 1024], F32),
            ("m", [128, 128], F32),
        ])
        dW = p.dma("sync", A["wstage"][:, :, :],
                   WD["L1_ffn_weight"].ap().rearrange("(r p) c -> p r c", p=128), inc="sD")
        tAm = E("vector", lambda v: v.tensor_reduce(
            scal[:, 0:1], A["wstage"].rearrange("p a b -> p (a b)"), axis=AX,
            op=OP.max, apply_absolute_value=True), waits=[dW], inc="sV")
        tWm = cross_part_max(scal[:, 0:1], arow[0:1, 0:1], P["m"], [tAm])
        tWm2 = E("vector", lambda v: v.tensor_single_scalar(arow[0:1, 0:1], arow[0:1, 0:1],
                                                            1e-12, OP.max), waits=[tWm], inc="sV")
        tWb = bcast_scalar(scal[:, 1:2], arow[0:1, 0:1], P["m"][:, 0:1], [tWm2])
        tWr = E("vector", lambda v: v.reciprocal(scal[:, 2:3], scal[:, 1:2]), waits=[tWb],
                inc="sV")
        tWs = E("vector", lambda v: v.tensor_scalar(scal[:, 2:3], scal[:, 2:3], 448.0, None,
                                                    OP.mult), waits=[tWr], inc="sV")
        wq_tok = None
        for j in range(8):
            trs = None
            for r in range(8):
                trs = E("tensor", lambda t, j=j, r=r: t.transpose(
                    P["w"][:, r, :], A["wstage"][:, r, j * 128:(j + 1) * 128], ident_f[:, :]),
                    waits=T(tIf, dW, wq_tok), inc="sT")
            wq_tok = quant_chain(P["w"].rearrange("p a b -> p (a b)"), A["wq"][:, j, :],
                                 scal[:, 2:3], A["q1"], A["q2"], A["q3"], A["msk"],
                                 T(trs, tWs))
        dB = p.dma("sync", A["q1"][0:1, :], WD["L1_ffn_bias"].ap().rearrange("(a c) -> a c", a=1),
                   waits=[wq_tok], inc="sD")
        tb1 = E("tensor", lambda t: t.matmul(P["y"][:, 0:512], ones_row[:, :],
                                             A["q1"][0:1, 0:512], start=True, stop=True),
                waits=T(tOnes, dB), inc=None)
        tb2 = E("tensor", lambda t: t.matmul(P["y"][:, 512:1024], ones_row[:, :],
                                             A["q1"][0:1, 512:1024], start=True, stop=True),
                inc="sT", fence=True)
        tb3 = E("vector", lambda v: v.tensor_scalar(A["bias"][:, :], P["y"][:, :], 0.5, None,
                                                    OP.mult), waits=[tb2], inc="sV")
        xg_free = [None, None]
        am_tok = None
        for g in range(NG):
            xg = A["xg"][:, g % 2]
            dX = p.dma("sync", xg, x_re(src)[g], waits=T(xg_free[g % 2], tb3, wq_tok), inc="sD")
            for t in range(TG):
                t8 = ln_tile(xg[:, t, :], A["xf"], 8 * ((TG * g + t) % 8), T(dX, am_tok))
                tr = E("vector", lambda v: v.tensor_reduce(scal[:, 4:5], A["xf"][:, :], axis=AX,
                                                           op=OP.max, apply_absolute_value=True),
                       waits=[t8], inc="sV")
                if am_tok is None:
                    am_tok = E("vector", lambda v: v.tensor_copy(scal[:, 5:6], scal[:, 4:5]),
                               waits=[tr], inc="sV")
                else:
                    am_tok = E("vector", lambda v: v.tensor_tensor(scal[:, 5:6], scal[:, 5:6],
                                                                   scal[:, 4:5], OP.max),
                               waits=[tr], inc="sV")
            xg_free[g % 2] = am_tok
        tXm = cross_part_max(scal[:, 5:6], arow[0:1, 1:2], P["m"], [am_tok])
        tXm2 = E("vector", lambda v: v.tensor_single_scalar(arow[0:1, 1:2], arow[0:1, 1:2],
                                                            1e-12, OP.max), waits=[tXm], inc="sV")
        dC1 = p.dma("gpsimd", cci[:, :], arow[0:1, 1:2], waits=[tXm2], inc="sDg")
        tCC = E("gpsimd", lambda g_: g_.collective_compute(
            "AllReduce", OP.max, replica_groups=[list(range(8))],
            ins=[cci.ap()], outs=[cco.ap()]), waits=[dC1], inc="sC")
        dC2 = p.dma("scalar", arow[0:1, 2:3], cco[:, :], waits=[tCC], inc="sD")
        tXb = bcast_scalar(scal[:, 6:7], arow[0:1, 2:3], P["m"][:, 0:1], [dC2])
        tXr = E("vector", lambda v: v.reciprocal(scal[:, 7:8], scal[:, 6:7]), waits=[tXb],
                inc="sV")
        tXs = E("vector", lambda v: v.tensor_scalar(scal[:, 7:8], scal[:, 7:8], 448.0, None,
                                                    OP.mult), waits=[tXr], inc="sV")
        tE1 = E("vector", lambda v: v.tensor_tensor(scal[:, 8:9], scal[:, 6:7], scal[:, 1:2],
                                                    OP.mult), waits=[tXs, tWb], inc="sV")
        tE2 = E("vector", lambda v: v.tensor_scalar(scal[:, 8:9], scal[:, 8:9],
                                                    float(0.5 / (448.0 * 448.0)), None, OP.mult),
                waits=[tE1], inc="sV")
        dO = None
        mm_prev = None
        tq = None
        to1 = None
        te = None
        for g in range(NG):
            xg = A["xg"][:, g % 2]
            dX = p.dma("sync", xg, x_re(src)[g], waits=T(xg_free[g % 2], tE2), inc="sD")
            out_toks = []
            for t in range(TG):
                t8 = ln_tile(xg[:, t, :], A["xf"], 8 * ((TG * g + t) % 8), T(dX, tq))
                tq = quant_chain(A["xf"][:, :], A["xb"][:, :], scal[:, 7:8],
                                 A["q1"], A["q2"], A["q3"], A["msk"],
                                 T(t8, tXs, wq_tok, tb3, te))
                te = transposes_to(P["t"], A["xb"], A["xT"][:, :, :], [tq], extra=T(mm_prev))
                mm = None
                for cc in range(8):
                    for h in range(2):
                        last = (cc == 7 and h == 1)
                        mm = E("tensor", lambda tt, cc=cc, h=h: tt.matmul(
                            P["y"][:, h * 512:(h + 1) * 512], A["xT"][:, cc, :],
                            A["wq"][:, cc, h * 512:(h + 1) * 512],
                            start=(cc == 0), stop=(cc == 7)), waits=T(te, to1, tb3),
                            inc="sT" if last else None, fence=last)
                mm_prev = mm
                to1 = E("vector", lambda v, t=t, xg=xg: v.scalar_tensor_tensor(
                    xg[:, t, :], P["y"][:, :], scal[:, 8:9], xg[:, t, :], OP.mult, OP.add),
                    waits=T(mm, tE2, tb3), inc="sV")
                to2 = E("vector", lambda v, t=t, xg=xg: v.tensor_tensor(
                    xg[:, t, :], xg[:, t, :], A["bias"][:, :], OP.add),
                    waits=[to1], inc="sV")
                out_toks.append(to2)
            dO = p.dma("scalar", x_re(dst)[g], xg, waits=out_toks, inc="sDo")
            xg_free[g % 2] = dO
        return dO

    def phase_l1attn(tag, src, dst):
        A = arena(tag, [
            ("wqkv", [128, 8, 24, 128], BF16),
            ("wstage", [128, 1024], F32),
            ("wstage2", [128, 8, 3, 2, 64], F32),
            ("xg", [128, 2, TG, 1024], F32),
            ("x0", [128, 1024], F32),
            ("xb", [128, 1024], BF16),
            ("xT", [128, 8, 512], BF16),
            ("st", [128, 6, 512], BF16),
            ("vst2", [128, 2, 128], BF16),
            ("ktb", [128, 2, S], BF16),
            ("vtb", [128, 2, S // 128, 128], BF16),
            ("qtg", [128, 2, 8, 512], BF16),
            ("PT", [128, 2, 1024], BF16),
            ("oT", [128, 8, 512], BF16),
            ("rr", [128, 512], F32),
            ("wproj", [128, 8, 1024], BF16),
            ("bpr", [128, 1024], F32),
        ])
        P1 = psum(tag + "1", [
            ("t", [128, 8, 128], BF16),
            ("s2", [128, 2, 512], F32),
        ])
        psTv = P1["t"]
        P = psum(tag + "2", [
            ("s", [128, 2, 1024], F32),
            ("o", [128, 512], F32),
            ("n", [128, 512], F32),
            ("y", [128, 1024], F32),
        ])
        wq_toks = []
        dprev = None
        for cc in range(8):
            dW = None
            for k in range(3):
                for i in range(2):
                    dW = p.dma("sync", A["wstage2"][:, :, k, i, :],
                               bass.AP(WD["L1_attn_W_qkv"],
                                       cc * 128 * 3072 + 64 * k + 192 * i,
                                       [[3072, 128], [384, 8], [1, 64]]),
                               waits=T(dprev), inc="sD")
            dprev = E("vector", lambda v, cc=cc: v.tensor_copy(
                A["wqkv"][:, cc, :, :].rearrange("p a b -> p (a b)"),
                A["wstage2"].rearrange("p a b c d -> p (a b c d)")), waits=[dW], inc="sV")
            wq_toks.append(dprev)
        bq_toks = []
        for pr in range(8):
            for k in range(3):
                bq_toks.append(p.dma("sync", bqkv_col[:, pr * 3 + k:pr * 3 + k + 1],
                                     bass.AP(WD["L1_attn_b_qkv"], 384 * pr + 64 * k,
                                             [[192, 2], [1, 64]]), inc="sD"))
        WQ = A["wqkv"]
        xg_free = [None, None]
        st_free = [None] * 6
        ev_last = [None, None]
        vst2_free = [None, None]
        tr_prev = None
        qkv_dma = []
        for g in range(NG):
            xg = A["xg"][:, g % 2]
            dX = p.dma("sync", xg, x_re(src)[g], waits=T(xg_free[g % 2], *wq_toks[-2:]), inc="sD")
            tr_evac = []
            t9 = None
            te = None
            for t in range(TG):
                t9 = ln_tile(xg[:, t, :], A["x0"], 8 * ((TG * g + t) % 8), T(dX, t9, te),
                             bf_out=A["xb"])
                te = transposes_to(psTv, A["xb"], A["xT"][:, :, 128 * t:128 * (t + 1)],
                                   [t9], extra=T(tr_evac[-1] if tr_evac else None, tr_prev,
                                                 *wq_toks[-1:]))
                tr_evac.append(te)
            tr_prev = tr_evac[-1]
            for pr in range(8):
                for k in range(3):
                    b_ = (pr * 3 + k) % 2
                    mm = None
                    for cc in range(8):
                        mm = E("tensor", lambda t, pr=pr, k=k, cc=cc, b_=b_: t.matmul(
                            P1["s2"][:, b_, :], WQ[:, cc, pr * 3 + k, :],
                            A["xT"][:, cc, :], start=(cc == 0), stop=(cc == 7)),
                            waits=T(tr_evac[-1], wq_toks[-1], ev_last[b_]),
                            inc="sT" if cc == 7 else None, fence=(cc == 7))
                    bci = pr * 3 + k
                    si = k * 2 + (pr % 2)
                    ev = E("vector", lambda v, b_=b_, bci=bci, si=si: v.tensor_scalar(
                        A["st"][:, si, :], P1["s2"][:, b_, :], bqkv_col[:, bci:bci + 1],
                        None, OP.add), waits=T(mm, bq_toks[-1], st_free[si]), inc="sV")
                    ev_last[b_] = ev
                    if k == 0:
                        d_ = p.dma("scalar", bass.AP(qTd, pr * 128 * S + 512 * g,
                                                   [[S, 128], [1, 512]]),
                                   A["st"][:, si, :], waits=[ev], inc="sDo")
                        st_free[si] = d_
                    elif k == 1:
                        d_ = p.dma("scalar", bass.AP(ktd, pr * 128 * S + 512 * g,
                                                   [[S, 128], [1, 512]]),
                                   A["st"][:, si, :], waits=[ev], inc="sDo")
                        st_free[si] = d_
                    else:
                        tt = None
                        tv = None
                        for sx in range(TG):
                            tt = E("tensor", lambda t, sx=sx, si=si: t.transpose(
                                psTv[:, sx, :], A["st"][:, si, 128 * sx:128 * (sx + 1)],
                                ident_b[:, :]), waits=T(ev, tIb, tr_prev, tv), inc="sT",
                                fence=True)
                            tv = E("vector", lambda v, sx=sx: v.tensor_copy(
                                A["vst2"][:, sx % 2, :],
                                psTv[:, sx, :]), waits=T(tt, vst2_free[sx % 2]), inc="sV")
                            d_ = p.dma("scalar",
                                       bass.AP(vtd, pr * 128 * S + (4 * g + sx) * 128,
                                               [[S, 128], [1, 128]]),
                                       A["vst2"][:, sx % 2, :], waits=[tv], inc="sDo")
                            vst2_free[sx % 2] = d_
                        st_free[si] = tt
                        tr_prev = vst2_free[(TG - 1) % 2]
                    qkv_dma.append(st_free[si])
            xg_free[g % 2] = tr_evac[-1]
        d1_dma_done = ("sDo", p.counts["sDo"])
        wp_tok = None
        for j in range(8):
            dW = p.dma("sync", A["wstage"][:, :],
                       bass.AP(WD["L1_attn_W_proj"], j * 128 * 1024, [[1024, 128], [1, 1024]]),
                       waits=T(wp_tok), inc="sD")
            wp_tok = E("vector", lambda v, j=j: v.tensor_copy(A["wproj"][:, j, :],
                                                              A["wstage"][:, :]),
                       waits=[dW], inc="sV")
        dB = p.dma("sync", A["wstage"][0:1, :],
                   WD["L1_attn_b_proj"].ap().rearrange("(a c) -> a c", a=1), inc="sD")
        tb1 = E("tensor", lambda t: t.matmul(P["y"][:, 0:512], ones_row[:, :],
                                             A["wstage"][0:1, 0:512], start=True, stop=True),
                waits=T(tOnes, dB, tr_prev), inc=None)
        tb2 = E("tensor", lambda t: t.matmul(P["y"][:, 512:1024], ones_row[:, :],
                                             A["wstage"][0:1, 512:1024], start=True, stop=True),
                inc="sT", fence=True)
        tb3 = E("vector", lambda v: v.tensor_copy(A["bpr"][:, :], P["y"][:, :]),
                waits=[tb2], inc="sV")
        dO = None
        xg_free2 = None
        qtg_free = [None, None]
        ktb_free = [None, None]
        oT_free = None
        y_mm_prev = None
        exp_last = [None, None]
        pt_last = [None, None]
        y_mm_last_qg = None
        to1_qg_prev = None
        ptS_free = d1_dma_done
        for qg in range(NG):
            qb = qg % 2
            dQ = p.dma("sync", A["qtg"][:, qb, :, :],
                       bass.AP(qTd, 512 * qg, [[S, 128], [128 * S, 8], [1, 512]]),
                       waits=T(qtg_free[qb], d1_dma_done), inc="sD")
            dX = p.dma("sync", A["xg"][:, 0], x_re(src)[qg], waits=T(xg_free2, d1_dma_done),
                       inc="sD")
            pair_out = []
            for pr in range(8):
                kb = pr % 2
                dK = p.dma("sync", A["ktb"][:, kb, :],
                           bass.AP(ktd, pr * 128 * S, [[S, 128], [1, S]]),
                           waits=T(ktb_free[kb], d1_dma_done), inc="sD")
                dV = p.dma("sync", A["vtb"][:, kb, :, :].rearrange("p a b -> p (a b)"),
                           bass.AP(vtd, pr * 128 * S, [[S, 128], [1, S]]),
                           waits=T(ktb_free[kb]), inc="sD")
                NC_ = S // 128
                expD = [None] * NC_
                pvB_toks = [None] * NC_
                add_toks = [None] * (NC_ // 2)
                nB_last = None
                for sc in range(NC_ + 1):
                    if sc < NC_:
                        par = sc % 2
                        E("tensor", lambda t, sc=sc, par=par, qb=qb, pr=pr, kb=kb: t.matmul(
                            P["s"][:, par, 0:512], A["ktb"][0:64, kb, 128 * sc:128 * (sc + 1)],
                            A["qtg"][0:64, qb, pr, :], start=True, stop=True,
                            tile_position=(0, 0)),
                            waits=T(dK, dQ, dV, expD[sc - 2] if sc >= 2 else None,
                                    y_mm_prev), inc=None)
                        m2 = E("tensor", lambda t, sc=sc, par=par, qb=qb, pr=pr, kb=kb: t.matmul(
                            P["s"][:, par, 512:1024],
                            A["ktb"][64:128, kb, 128 * sc:128 * (sc + 1)],
                            A["qtg"][64:128, qb, pr, :], start=True, stop=True,
                            tile_position=(64, 0), skip_group_check=True),
                            inc="sT", fence=True)
                        E("scalar", lambda s, sc=sc, par=par: s.activation(
                            A["PT"][:, sc % 2, :], P["s"][:, par, :], AF.Exp),
                            waits=T(m2, pvB_toks[sc - 2] if sc >= 2 else None,
                                    add_toks[(sc - 2) // 2] if sc >= 2 else None), inc="sS")
                        expD[sc] = E("scalar", lambda s: s.drain(), inc="sS")
                        if par == 1:
                            j = sc // 2
                            add_toks[j] = E("vector", lambda v: v.tensor_tensor(
                                A["st"][:, 0:2, :].rearrange("p a b -> p (a b)"),
                                A["PT"][:, 0, :], A["PT"][:, 1, :], OP.add),
                                waits=T(expD[sc], nB_last if j > 0 else ptS_free),
                                inc="sV")
                    if sc >= 1:
                        sp = sc - 1
                        E("tensor", lambda t, sp=sp, pr=pr, kb=kb: t.matmul(
                            P["o"][0:64, :], A["vtb"][:, kb, sp, 0:64],
                            A["PT"][:, sp % 2, 0:512], start=(sp == 0),
                            stop=(sp == NC_ - 1), tile_position=(0, 0)),
                            waits=T(expD[sp], dV, oT_free), inc=None)
                        pvB_toks[sp] = E("tensor", lambda t, sp=sp, pr=pr, kb=kb: t.matmul(
                            P["o"][64:128, :], A["vtb"][:, kb, sp, 64:128],
                            A["PT"][:, sp % 2, 512:1024], start=(sp == 0),
                            stop=(sp == NC_ - 1), tile_position=(0, 64),
                            skip_group_check=True), inc="sT")
                        if sp % 2 == 1:
                            j = sp // 2
                            E("tensor", lambda t, j=j: t.matmul(
                                P["n"][0:64, :], ones64[:, :], A["st"][:, 0, :],
                                start=(j == 0), stop=(j == NC_ // 2 - 1),
                                tile_position=(0, 0), skip_group_check=True),
                                waits=T(tO64, add_toks[j]), inc=None)
                            nB_last = E("tensor", lambda t, j=j: t.matmul(
                                P["n"][64:128, :], ones64[:, :], A["st"][:, 1, :],
                                start=(j == 0), stop=(j == NC_ // 2 - 1),
                                tile_position=(0, 64), skip_group_check=True),
                                inc="sT", fence=(j == NC_ // 2 - 1))
                ptS_free = nB_last
                ktb_free[kb] = nB_last
                trr = E("vector", lambda v: v.reciprocal(A["rr"][:, :], P["n"][:, :]),
                        waits=[nB_last], inc="sV")
                tev = E("vector", lambda v, pr=pr: v.tensor_tensor(
                    A["oT"][:, pr, :], P["o"][:, :], A["rr"][:, :], OP.mult),
                    waits=T(trr, y_mm_last_qg), inc="sV")
                oT_free = tev
                pair_out.append(tev)
            qtg_free[qb] = pair_out[-1]
            out_toks = []
            to1 = None
            for t in range(TG):
                mm = None
                for pr in range(8):
                    for h in range(2):
                        last = (pr == 7 and h == 1)
                        mm = E("tensor", lambda tt, t=t, pr=pr, h=h: tt.matmul(
                            P["y"][:, h * 512:(h + 1) * 512],
                            A["oT"][:, pr, 128 * t:128 * (t + 1)],
                            A["wproj"][:, pr, h * 512:(h + 1) * 512],
                            start=(pr == 0), stop=(pr == 7)),
                            waits=T(pair_out[-1], wp_tok, to1, to1_qg_prev, tb3),
                            inc="sT" if last else None, fence=last)
                y_mm_prev = mm
                y_mm_last_qg = mm
                to1 = E("vector", lambda v, t=t: v.tensor_tensor(
                    A["xg"][:, 0, t, :], A["xg"][:, 0, t, :], P["y"][:, :], OP.add),
                    waits=T(mm, dX, tb3), inc="sV")
                to2 = E("vector", lambda v, t=t: v.tensor_tensor(
                    A["xg"][:, 0, t, :], A["xg"][:, 0, t, :], A["bpr"][:, :], OP.add),
                    waits=[to1], inc="sV")
                out_toks.append(to2)
            to1_qg_prev = to1
            dO = p.dma("scalar", x_re(dst)[qg], A["xg"][:, 0], waits=out_toks, inc="sDo")
            xg_free2 = dO
        return dO

    chain_src = [x_in, xs[0], xs[1], xs[2], xs[3], xs[4]]
    chain_dst = [xs[0], xs[1], xs[2], xs[3], xs[4], y_out]
    if n_sub < 6:
        chain_dst[n_sub - 1] = y_out
    phases = [
        lambda i: phase_l0ffn("A", chain_src[0], chain_dst[0]),
        lambda i: phase_l0attn("B", chain_src[1], chain_dst[1]),
        lambda i: phase_l0ffn("A2", chain_src[2], chain_dst[2]),
        lambda i: phase_l1ffn("CC", chain_src[3], chain_dst[3], *ccio[0]),
        lambda i: phase_l1attn("D", chain_src[4], chain_dst[4]),
        lambda i: phase_l1ffn("EE", chain_src[5], chain_dst[5], *ccio[1]),
    ]
    for i in range(n_sub):
        phases[i](i)
        p.barrier()
    E("sync", lambda e: e.nop(), waits=[("sDo", p.counts["sDo"]), ("sDg", p.counts["sDg"])])

    sems = {n_: nc.alloc_semaphore(n_) for n_ in
            ("sD", "sDo", "sDg", "sV", "sS", "sT", "sG", "sC")}
    with nc.Block() as block:
        p.build_into(block, sems)
    return nc


def kernel(**inputs):
    x_all = np.ascontiguousarray(inputs["L0_attn_x"], dtype=np.float32)
    weights = {k: np.ascontiguousarray(v, dtype=np.float32) for k, v in inputs.items()
               if k != "L0_attn_x"}
    nc = build_program()
    in_maps = [dict(x=x_all[i], **weights) for i in range(8)]
    res = run_bass_kernel_spmd(nc, in_maps, core_ids=list(range(8)))
    return np.stack([np.asarray(res.results[i]["y"], dtype=np.float32) for i in range(8)],
                    axis=0)


# version 11
# speedup vs baseline: 1.0344x; 1.0344x over previous
import numpy as np
import concourse.bass as bass
import concourse.mybir as mybir
from concourse.alu_op_type import AluOpType as OP
from concourse.bass_utils import run_bass_kernel_spmd

F32 = mybir.dt.float32
BF16 = mybir.dt.bfloat16
U8 = mybir.dt.uint8
AF = mybir.ActivationFunctionType
AX = mybir.AxisListType.X

S = 4096
C = 1024
NG = 8
TG = 4
EPS = 1e-5
ENGS = ("tensor", "vector", "scalar", "gpsimd", "sync")


class Prog:

    def __init__(self):
        self.counts = {}
        self.ops = {e: [] for e in ENGS}
        self.lastw = {e: {} for e in ENGS}
        for s_ in ("sD", "sDo", "sDg", "sV", "sS", "sT", "sG", "sC"):
            self.counts[s_] = 0

    def emit(self, eng, build, waits=(), inc=None, n=1, fence=False):
        w = []
        for t in waits:
            if t is None:
                continue
            s, v = t
            if self.lastw[eng].get(s, 0) >= v:
                continue
            self.lastw[eng][s] = v
            w.append((s, v))
        tok = None
        if inc is not None:
            self.counts[inc] += n
            tok = (inc, self.counts[inc])
        self.ops[eng].append((w, build, inc, n, fence))
        return tok

    def dma(self, eng, out, in_, waits=(), inc="sD"):
        return self.emit(eng, lambda e: e.dma_start(out, in_), waits, inc, 16)

    def barrier(self):
        toks = [(s, v) for s, v in self.counts.items() if v > 0]
        for eng in ENGS:
            self.emit(eng, lambda e: e.nop(), waits=toks)

    def build_into(self, block, sems):
        for eng in ENGS:
            ops = self.ops[eng]
            if not ops:
                continue

            def body(e, ops=ops, eng=eng):
                for waits, build, inc, n, fence in ops:
                    for s, v in waits:
                        e.wait_ge(sems[s], v)
                    ins = build(e)
                    if inc is not None:
                        if fence:
                            e.drain().then_inc(sems[inc], n)
                        else:
                            ins.then_inc(sems[inc], n)

            getattr(block, eng)(body)


def T(*toks):
    return [t for t in toks if t is not None]


def build_program(n_sub=6):
    nc = bass.Bass("TRN2", target_bir_lowering=False, detect_race_conditions=False)

    x_in = nc.dram_tensor("x", [S, C], F32, kind="ExternalInput")
    y_out = nc.dram_tensor("y", [S, C], F32, kind="ExternalOutput")
    WD = {}
    for nm, shp in [
        ("L0_attn_W_q", [C, C]), ("L0_attn_W_k", [C, C]), ("L0_attn_W_v", [C, C]),
        ("L0_attn_W_o", [C, C]), ("L0_ffn_W_r", [C, C]), ("L0_ffn_W_k", [C, C]),
        ("L0_ffn_W_v", [C, C]), ("L0_ffn_mu_r", [C]), ("L0_ffn_mu_k", [C]),
        ("L1_attn_W_qkv", [C, 3 * C]), ("L1_attn_b_qkv", [3 * C]),
        ("L1_attn_W_proj", [C, C]), ("L1_attn_b_proj", [C]),
        ("L1_ffn_weight", [C, C]), ("L1_ffn_bias", [C]),
    ]:
        WD[nm] = nc.dram_tensor(nm, shp, F32, kind="ExternalInput")
    xs = [nc.dram_tensor(f"xs{i}", [S, C], F32) for i in range(1, 6)]
    qTd = nc.dram_tensor("qTd", [8, 128, S], BF16)
    ktd = nc.dram_tensor("ktd", [8, 128, S], BF16)
    vtd = nc.dram_tensor("vtd", [8, 128, S // 128, 128], BF16)
    ccio = [(nc.dram_tensor(f"cci{i}", [1, 1], F32), nc.dram_tensor(f"cco{i}", [1, 1], F32))
            for i in range(2)]

    p = Prog()
    E = p.emit

    sb = nc.alloc_sbuf_tensor
    ident_b = sb("ident_b", [128, 128], BF16)
    ident_f = sb("ident_f", [128, 128], F32)
    i0 = sb("i0", [128, 128], F32)
    ones_row = sb("ones_row", [1, 128], F32)
    ones64 = sb("ones64", [128, 64], BF16)
    cols = sb("cols", [128, 64], F32)
    lnjunk = sb("lnjunk", [128, C], F32)
    mu_sb = sb("mu_sb", [128, 16], F32)
    bqkv_col = sb("bqkv_col", [128, 24], F32)
    scal = sb("scal", [128, 16], F32)
    arow = sb("arow", [1, 8], F32)
    PERSIST_BASE = nc.sbuf_base
    SBUF_TOP = nc.sbuf_top

    E("gpsimd", lambda g: g.iota(i0[:, :], [[1, 128]], channel_multiplier=0,
                                 allow_small_or_imprecise_dtypes=True))
    tIo = E("gpsimd", lambda g: g.iota(ident_f[:, :], [[0, 128]], channel_multiplier=1,
                                       allow_small_or_imprecise_dtypes=True), inc="sG")
    tIb = E("vector", lambda v: v.tensor_tensor(ident_b[:, :], i0[:, :], ident_f[:, :],
                                                OP.is_equal), waits=[tIo], inc="sV")
    tIf = E("vector", lambda v: v.tensor_tensor(ident_f[:, :], i0[:, :], ident_f[:, :],
                                                OP.is_equal), waits=[tIb], inc="sV")
    tOnes = E("gpsimd", lambda g: g.memset(ones_row[:, :], 1.0), inc="sG")
    tO64 = E("gpsimd", lambda g: g.memset(ones64[:, :], 1.0), inc="sG")

    def arena(tag, items):
        nc.sbuf_base = PERSIST_BASE
        out = {}
        for nm, shp, dt in items:
            out[nm] = nc.alloc_sbuf_tensor(tag + nm, shp, dt).ap()
        assert nc.sbuf_base <= SBUF_TOP, f"SBUF overflow {tag}: {nc.sbuf_base}>{SBUF_TOP}"
        return out

    def psum(tag, items):
        nc.psum_base = 0
        out = {}
        for nm, shp, dt in items:
            h = nc.alloc_psum_tensor(tag + nm, shp, dt)
            out[nm] = h.ap()
            out[nm + "_h"] = h
        assert nc.psum_base <= 8
        return out

    def ln_tile(xg_tile, x0_out, ci, tok_in, bf_out=None):
        c8 = cols[:, ci:ci + 8]
        t1 = E("vector", lambda v: v.reduce_sum(c8[:, 0:1], xg_tile, axis=AX),
               waits=tok_in, inc="sV")
        t2 = E("vector", lambda v: v.tensor_scalar(c8[:, 1:2], c8[:, 0:1], -1.0 / C, None,
                                                   OP.mult), waits=[t1], inc="sV")
        t3 = E("scalar", lambda s: s.activation(x0_out, xg_tile, AF.Identity, bias=c8[:, 1:2],
                                                scale=1.0), waits=T(t2, *tok_in), inc="sS")
        t4 = E("scalar", lambda s: s.activation(lnjunk[:, :], xg_tile, AF.Square,
                                                bias=c8[:, 1:2], scale=1.0,
                                                accum_out=c8[:, 2:3]), inc="sS")
        t5 = E("vector", lambda v: v.tensor_scalar(c8[:, 3:4], c8[:, 2:3], 1.0 / C, EPS,
                                                   OP.mult, op1=OP.add), waits=[t4], inc="sV")
        t6 = E("scalar", lambda s: s.activation(c8[:, 4:5], c8[:, 3:4], AF.Sqrt),
               waits=[t5], inc="sS")
        t7 = E("vector", lambda v: v.reciprocal(c8[:, 5:6], c8[:, 4:5]), waits=[t6], inc="sV")
        t8 = E("vector", lambda v: v.tensor_scalar(x0_out, x0_out, c8[:, 5:6], None, OP.mult),
               waits=[t7, t3], inc="sV")
        if bf_out is None:
            return t8
        return E("vector", lambda v: v.tensor_copy(bf_out, x0_out), waits=[t8], inc="sV")

    def transposes_to(psT, src_bf, dst_sb, tok_in, extra=()):
        trs = None
        for j in range(8):
            trs = E("tensor", lambda t, j=j: t.transpose(psT[:, j, :],
                                                         src_bf[:, j * 128:(j + 1) * 128],
                                                         ident_b[:, :]),
                    waits=T(tIb, *tok_in, *extra), inc="sT" if j == 7 else None, fence=(j == 7))
        return E("vector", lambda v: v.tensor_copy(dst_sb, psT[:, :, :]), waits=[trs], inc="sV")

    def cross_part_max(col1, out11, ps_m, tok_in):
        tt = E("tensor", lambda t: t.transpose(ps_m[0:1, 0:128], col1, ident_f[:, :]),
               waits=T(tIf, *tok_in), inc="sT", fence=True)
        return E("vector", lambda v: v.reduce_max(out11, ps_m[0:1, 0:128], axis=AX),
                 waits=[tt], inc="sV")

    def bcast_scalar(dst_col, src11, ps_col, tok_in):
        tm = E("tensor", lambda t: t.matmul(ps_col, ones_row[:, :], src11, start=True,
                                            stop=True), waits=T(tOnes, *tok_in), inc="sT",
               fence=True)
        return E("vector", lambda v: v.tensor_copy(dst_col, ps_col), waits=[tm], inc="sV")

    def prep_wT(wdram, dst, wstage, psW, scale, extra=()):
        dW = p.dma("sync", wstage[:, :, :], wdram.ap().rearrange("(r p) c -> p r c", p=128),
                   waits=list(extra), inc="sD")
        tprev = None
        for j in range(8):
            trs = None
            for r in range(8):
                trs = E("tensor", lambda t, j=j, r=r: t.transpose(
                    psW[:, r, :], wstage[:, r, j * 128:(j + 1) * 128], ident_f[:, :]),
                    waits=T(tIf, dW, tprev), inc="sT" if r == 7 else None, fence=(r == 7))
            tprev = E("vector", lambda v, j=j: v.tensor_scalar(dst[:, j, :], psW[:, :, :],
                                                               scale, None, OP.mult),
                      waits=[trs], inc="sV")
        return tprev, dW

    def quant_chain(src_ap, dst_bf, scol, q1, q2, q3, msk, tok_in):
        C20 = float(2 ** 20 + 1)
        MM = float(1.5 * 2 ** 14)
        E("vector", lambda v: v.tensor_scalar(q1, src_ap, scol, None, OP.mult), waits=tok_in)
        E("vector", lambda v: v.tensor_scalar(q2, q1, C20, None, OP.mult))
        E("vector", lambda v: v.tensor_tensor(q3, q2, q1, OP.subtract))
        E("vector", lambda v: v.tensor_tensor(q2, q2, q3, OP.subtract))
        E("vector", lambda v: v.tensor_tensor(q3, q1, q1, OP.mult))
        E("vector", lambda v: v.tensor_single_scalar(msk, q3, float(2.0 ** -12), OP.is_ge))
        E("vector", lambda v: v.tensor_scalar(q3, q1, MM, None, OP.add))
        E("vector", lambda v: v.tensor_scalar(q3, q3, MM, None, OP.subtract))
        return E("vector", lambda v: v.select(dst_bf, msk, q2, q3), inc="sV")

    x_re = lambda d: d.ap().rearrange("(g t p) c -> g p t c", p=128, t=TG)

    def phase_l0ffn(tag, src, dst):
        A = arena(tag, [
            ("wt", [128, 8, 3072], BF16),
            ("wstage", [128, 8, 1024], F32),
            ("xg", [128, 2, TG, 1024], F32),
            ("x0", [128, 2, 1024], F32),
            ("xb", [128, 2, 1024], BF16),
            ("xT", [128, 8, 513], BF16),
            ("dmix", [128, 512], BF16),
            ("xrT", [128, 8, 512], BF16),
            ("xkT", [128, 8, 512], BF16),
            ("kact", [128, 8, 512], BF16),
            ("relu", [128, 512], F32),
            ("rsb", [128, 2, 1024], F32),
            ("tmp", [128, 2, 1024], F32),
        ])
        P = psum(tag, [
            ("t", [128, 8, 128], BF16),
            ("k", [128, 2, 512], F32),
            ("r", [128, 1024], F32),
            ("v", [128, 1024], F32),
        ])
        psW = P["r"].rearrange("p (a b) -> p a b", a=8)
        wt = A["wt"]
        tw0, _ = prep_wT(WD["L0_ffn_W_r"], wt[:, :, 0:1024], A["wstage"], psW, 1.0)
        tw1, _ = prep_wT(WD["L0_ffn_W_k"], wt[:, :, 1024:2048], A["wstage"], psW, 1.0, [tw0])
        tw2, _ = prep_wT(WD["L0_ffn_W_v"], wt[:, :, 2048:3072], A["wstage"], psW, 0.5, [tw1])
        dmu = p.emit("gpsimd", lambda g: g.dma_start(
            mu_sb[:, 0:8], bass.AP(WD["L0_ffn_mu_r"], 0, [[1, 128], [128, 8]]),
            allow_slow_non_contiguous=True), inc="sDg", n=16)
        dmu2 = p.emit("gpsimd", lambda g: g.dma_start(
            mu_sb[:, 8:16], bass.AP(WD["L0_ffn_mu_k"], 0, [[1, 128], [128, 8]]),
            allow_slow_non_contiguous=True), inc="sDg", n=16)
        tb0 = E("vector", lambda v: v.memset(A["xT"][:, :, 0:1], 0.0), inc="sV")
        wtoks = [tw0, tw1, tw2, dmu, dmu2, tb0]

        xg_free = [None, None]
        bound_tok = tb0
        mix_last = None
        v_mm_last = None
        r_mm_last = None
        tsig_prev = dmu
        tm1_hist = [None, None]
        tm1_prev = None
        t9h = [None, None]
        teh = [None, None]
        dO = None
        for g in range(NG):
            xg = A["xg"][:, g % 2]
            dX = p.dma("sync", xg, x_re(src)[g], waits=T(xg_free[g % 2], *wtoks), inc="sD")
            if g > 0:
                bound_tok = E("vector", lambda v: v.tensor_copy(A["xT"][:, :, 0:1],
                                                                A["xT"][:, :, 512:513]),
                              waits=T(mix_last), inc="sV")
            tr_evac = []
            for t in range(TG):
                t9 = ln_tile(xg[:, t, :], A["x0"][:, t % 2, :], 8 * ((TG * g + t) % 8),
                             T(dX, t9h[t % 2], teh[t % 2]), bf_out=A["xb"][:, t % 2, :])
                t9h[t % 2] = t9
                te = transposes_to(P["t"], A["xb"][:, t % 2, :], A["xT"].rearrange(
                    "p c m -> p c m")[:, :, 1 + 128 * t:1 + 128 * (t + 1)],
                    [t9], extra=T(tr_evac[-1] if tr_evac else None, bound_tok, mix_last))
                teh[t % 2] = te
                tr_evac.append(te)
            mix_toks = []
            for cc in range(8):
                td = E("vector", lambda v, cc=cc: v.tensor_tensor(
                    A["dmix"], A["xT"][:, cc, 1:513], A["xT"][:, cc, 0:513 - 1], OP.subtract),
                    waits=T(tr_evac[-1], bound_tok, dmu, dmu2,
                            r_mm_last), inc="sV")
                tr_ = E("vector", lambda v, cc=cc: v.scalar_tensor_tensor(
                    A["xrT"][:, cc, :], A["dmix"], mu_sb[:, cc:cc + 1], A["xT"][:, cc, 0:512],
                    OP.mult, OP.add), waits=[td], inc="sV")
                tk_ = E("vector", lambda v, cc=cc: v.scalar_tensor_tensor(
                    A["xkT"][:, cc, :], A["dmix"], mu_sb[:, 8 + cc:9 + cc], A["xT"][:, cc, 0:512],
                    OP.mult, OP.add), inc="sV")
                mix_toks += [tr_, tk_]
            mix_last = mix_toks[-1]
            kact_toks = []
            for dch in range(8):
                mm = None
                for cc in range(8):
                    mm = E("tensor", lambda t, dch=dch, cc=cc: t.matmul(
                        P["k"][:, dch % 2, :], wt[:, cc, 1024 + dch * 128:1024 + (dch + 1) * 128],
                        A["xkT"][:, cc, :], start=(cc == 0), stop=(cc == 7)),
                        waits=T(mix_last, kact_toks[-2] if len(kact_toks) >= 2 else None,
                                v_mm_last), inc="sT" if cc == 7 else None, fence=(cc == 7))
                t1 = E("vector", lambda v, dch=dch: v.tensor_single_scalar(
                    A["relu"], P["k"][:, dch % 2, :], 0.0, OP.max), waits=[mm], inc="sV")
                t2 = E("vector", lambda v, dch=dch: v.tensor_tensor(
                    A["kact"][:, dch, :], A["relu"], A["relu"], OP.mult), waits=[t1], inc="sV")
                kact_toks.append(t2)
            out_toks = []
            for t in range(TG):
                mmr = None
                for cc in range(8):
                    for h in range(2):
                        last = (cc == 7 and h == 1)
                        mmr = E("tensor", lambda tt, t=t, cc=cc, h=h: tt.matmul(
                            P["r"][:, h * 512:(h + 1) * 512],
                            A["xrT"][:, cc, 128 * t:128 * (t + 1)],
                            wt[:, cc, h * 512:(h + 1) * 512],
                            start=(cc == 0), stop=(cc == 7)),
                            waits=T(tsig_prev), inc="sT" if last else None, fence=last)
                r_mm_last = mmr
                tsig = E("scalar", lambda s, t=t: s.activation(A["rsb"][:, t % 2, :], P["r"][:, :],
                                                               AF.Sigmoid),
                         waits=T(mmr, tm1_hist[t % 2]), inc="sS")
                tsig_prev = tsig
                mmv = None
                for dch in range(8):
                    for h in range(2):
                        last = (dch == 7 and h == 1)
                        mmv = E("tensor", lambda tt, t=t, dch=dch, h=h: tt.matmul(
                            P["v"][:, h * 512:(h + 1) * 512],
                            A["kact"][:, dch, 128 * t:128 * (t + 1)],
                            wt[:, dch, 2048 + h * 512:2048 + (h + 1) * 512],
                            start=(dch == 0), stop=(dch == 7)),
                            waits=T(kact_toks[-1], tm1_prev),
                            inc="sT" if last else None, fence=last)
                v_mm_last = mmv
                tm1 = E("vector", lambda v, t=t: v.tensor_tensor(
                    A["tmp"][:, t % 2, :], A["rsb"][:, t % 2, :], P["v"][:, :], OP.mult),
                    waits=T(mmv, tsig), inc="sV")
                tm1_hist[t % 2] = tm1
                tm1_prev = tm1
                tm2 = E("vector", lambda v, t=t, xg=xg: v.tensor_tensor(
                    xg[:, t, :], xg[:, t, :], A["tmp"][:, t % 2, :], OP.add),
                    waits=[tm1], inc="sV")
                out_toks.append(tm2)
            dO = p.dma("scalar", x_re(dst)[g], xg, waits=out_toks, inc="sDo")
            xg_free[g % 2] = dO
        return dO

    def phase_l0attn(tag, src, dst):
        A = arena(tag, [
            ("wt", [128, 8, 4096], BF16),
            ("wstage", [128, 8, 1024], F32),
            ("xg", [128, 2, TG, 1024], F32),
            ("x0", [128, 2, 1024], F32),
            ("xb", [128, 2, 1024], BF16),
            ("xT", [128, 8, 512], BF16),
            ("ksb", [128, TG, 1024], BF16),
            ("vsb", [128, TG, 1024], BF16),
            ("kvb", [128, 8, 128], BF16),
            ("aT", [128, 8, 512], BF16),
            ("qst", [128, 2, 512], BF16),
            ("qg", [128, 8, 512], BF16),
        ])
        P = psum(tag, [
            ("t", [128, 8, 128], BF16),
            ("q", [128, 2, 512], F32),
            ("k", [128, 1024], F32),
            ("v", [128, 1024], F32),
            ("kv", [128, 8, 64], F32),
        ])
        wt = A["wt"]
        tw0, _ = prep_wT(WD["L0_attn_W_q"], wt[:, :, 0:1024], A["wstage"], P["k"].rearrange(
            "p (a b) -> p a b", a=8), 0.125)
        tw1, _ = prep_wT(WD["L0_attn_W_k"], wt[:, :, 1024:2048], A["wstage"],
                         P["k"].rearrange("p (a b) -> p a b", a=8), 1.0, [tw0])
        tw2, _ = prep_wT(WD["L0_attn_W_v"], wt[:, :, 2048:3072], A["wstage"],
                         P["k"].rearrange("p (a b) -> p a b", a=8), 1.0, [tw1])
        tw3, _ = prep_wT(WD["L0_attn_W_o"], wt[:, :, 3072:4096], A["wstage"],
                         P["k"].rearrange("p (a b) -> p a b", a=8), 1.0, [tw2])
        wtoks = [tw0, tw1, tw2, tw3]
        xg_free = [None, None]
        kv_last = None
        q_dma = [None, None]
        qev_hist = [None, None]
        ksb_free = None
        t9h = [None, None]
        teh = [None, None]
        for g in range(NG):
            xg = A["xg"][:, g % 2]
            dX = p.dma("sync", xg, x_re(src)[g], waits=T(xg_free[g % 2], *wtoks), inc="sD")
            tr_evac = []
            for t in range(TG):
                t9 = ln_tile(xg[:, t, :], A["x0"][:, t % 2, :], 8 * ((TG * g + t) % 8),
                             T(dX, t9h[t % 2], teh[t % 2]), bf_out=A["xb"][:, t % 2, :])
                t9h[t % 2] = t9
                te = transposes_to(P["t"], A["xb"][:, t % 2, :],
                                   A["xT"][:, :, 128 * t:128 * (t + 1)],
                                   [t9], extra=T(tr_evac[-1] if tr_evac else None, *wtoks))
                teh[t % 2] = te
                tr_evac.append(te)
            q_toks = []
            for pr in range(8):
                mm = None
                for cc in range(8):
                    mm = E("tensor", lambda t, pr=pr, cc=cc: t.matmul(
                        P["q"][:, pr % 2, :], wt[:, cc, pr * 128:(pr + 1) * 128],
                        A["xT"][:, cc, :], start=(cc == 0), stop=(cc == 7)),
                        waits=T(tr_evac[-1], qev_hist[pr % 2]),
                        inc="sT" if cc == 7 else None, fence=(cc == 7))
                ev = E("vector", lambda v, pr=pr: v.tensor_copy(A["qst"][:, pr % 2, :],
                                                                P["q"][:, pr % 2, :]),
                       waits=T(mm, q_dma[pr % 2]), inc="sV")
                dq = p.dma("scalar", bass.AP(qTd, pr * 128 * S + 512 * g, [[S, 128], [1, 512]]),
                           A["qst"][:, pr % 2, :], waits=[ev], inc="sDo")
                q_dma[pr % 2] = dq
                qev_hist[pr % 2] = ev
                q_toks.append(ev)
            kv_sb_toks = []
            for t in range(TG):
                mmk = None
                for cc in range(8):
                    for h in range(2):
                        last = (cc == 7 and h == 1)
                        mmk = E("tensor", lambda tt, t=t, cc=cc, h=h: tt.matmul(
                            P["k"][:, h * 512:(h + 1) * 512],
                            A["xT"][:, cc, 128 * t:128 * (t + 1)],
                            wt[:, cc, 1024 + h * 512:1024 + (h + 1) * 512],
                            start=(cc == 0), stop=(cc == 7)),
                            waits=T(ksb_free), inc="sT" if last else None, fence=last)
                tk = E("vector", lambda v, t=t: v.tensor_copy(A["ksb"][:, t, :], P["k"][:, :]),
                       waits=[mmk], inc="sV")
                mmv = None
                for cc in range(8):
                    for h in range(2):
                        last = (cc == 7 and h == 1)
                        mmv = E("tensor", lambda tt, t=t, cc=cc, h=h: tt.matmul(
                            P["v"][:, h * 512:(h + 1) * 512],
                            A["xT"][:, cc, 128 * t:128 * (t + 1)],
                            wt[:, cc, 2048 + h * 512:2048 + (h + 1) * 512],
                            start=(cc == 0), stop=(cc == 7)),
                            inc="sT" if last else None, fence=last)
                tv = E("vector", lambda v, t=t: v.tensor_copy(A["vsb"][:, t, :], P["v"][:, :]),
                       waits=[mmv], inc="sV")
                kv_sb_toks += [tk, tv]
            for t in range(TG):
                for pr in range(8):
                    first = (g == 0 and t == 0 and pr == 0)
                    last = (g == NG - 1 and t == TG - 1 and pr == 7)
                    glast = (t == TG - 1 and pr == 7)
                    E("tensor", lambda tt, t=t, pr=pr, first=first, last=last: tt.matmul(
                        P["kv"][0:64, pr, :], A["ksb"][:, t, 128 * pr:128 * pr + 64],
                        A["vsb"][:, t, 128 * pr:128 * pr + 64],
                        start=first, stop=last, skip_group_check=True,
                        tile_position=(0, 0)), waits=T(kv_sb_toks[-1]), inc=None)
                    kv_last = E("tensor", lambda tt, t=t, pr=pr, first=first, last=last: tt.matmul(
                        P["kv"][64:128, pr, :], A["ksb"][:, t, 128 * pr + 64:128 * (pr + 1)],
                        A["vsb"][:, t, 128 * pr + 64:128 * (pr + 1)],
                        start=first, stop=last, skip_group_check=True,
                        tile_position=(0, 64)), inc="sT" if glast else None, fence=last)
            ksb_free = kv_last
            xg_free[g % 2] = tr_evac[-1]
        tz = E("vector", lambda v: v.memset(A["kvb"][:, :, :], 0.0), inc="sV")
        kv_ev = tz
        for pr in range(8):
            E("vector", lambda v, pr=pr: v.tensor_copy(A["kvb"][0:64, pr, 0:64],
                                                       P["kv"][0:64, pr, :]),
              waits=T(kv_last, tz))
            kv_ev = E("vector", lambda v, pr=pr: v.tensor_copy(A["kvb"][64:128, pr, 64:128],
                                                               P["kv"][64:128, pr, :]), inc="sV")
        psA2 = P["k"]
        psY = P["v"]
        qg_dma_free = None
        aev_hist = [None, None]
        to_prev = None
        dO = None
        for g in range(NG):
            xg = A["xg"][:, g % 2]
            dX = p.dma("sync", xg, x_re(src)[g], waits=T(xg_free[g % 2], kv_ev), inc="sD")
            dq = p.dma("sync", A["qg"][:, :, :],
                       bass.AP(qTd, 512 * g, [[S, 128], [128 * S, 8], [1, 512]]),
                       waits=T(qg_dma_free, kv_ev, q_dma[0], q_dma[1]), inc="sD")
            a_toks = []
            for pr in range(8):
                b_ = pr % 2
                ma = E("tensor", lambda t, pr=pr, b_=b_: t.matmul(
                    psA2[0:64, b_ * 512:(b_ + 1) * 512], A["kvb"][:, pr, 0:64],
                    A["qg"][:, pr, :], start=True, stop=True, tile_position=(0, 0)),
                    waits=T(kv_ev, dq, aev_hist[b_]), inc=None)
                mb = E("tensor", lambda t, pr=pr, b_=b_: t.matmul(
                    psA2[64:128, b_ * 512:(b_ + 1) * 512], A["kvb"][:, pr, 64:128],
                    A["qg"][:, pr, :], start=True, stop=True, tile_position=(0, 64),
                    skip_group_check=True), inc="sT", fence=True)
                te = E("vector", lambda v, pr=pr, b_=b_: v.tensor_copy(
                    A["aT"][:, pr, :], psA2[:, b_ * 512:(b_ + 1) * 512]),
                    waits=[mb], inc="sV")
                aev_hist[b_] = te
                a_toks.append(te)
            qg_dma_free = a_toks[-1]
            out_toks = []
            for t in range(TG):
                mm = None
                for pr in range(8):
                    for h in range(2):
                        last = (pr == 7 and h == 1)
                        mm = E("tensor", lambda tt, t=t, pr=pr, h=h: tt.matmul(
                            psY[:, h * 512:(h + 1) * 512], A["aT"][:, pr, 128 * t:128 * (t + 1)],
                            wt[:, pr, 3072 + h * 512:3072 + (h + 1) * 512],
                            start=(pr == 0), stop=(pr == 7)),
                            waits=T(a_toks[-1], to_prev), inc="sT" if last else None,
                            fence=last)
                to = E("vector", lambda v, t=t, xg=xg: v.tensor_tensor(
                    xg[:, t, :], xg[:, t, :], psY[:, :], OP.add),
                    waits=T(mm, dX), inc="sV")
                to_prev = to
                out_toks.append(to)
            dO = p.dma("scalar", x_re(dst)[g], xg, waits=out_toks, inc="sDo")
            xg_free[g % 2] = dO
        return dO

    def phase_l1ffn(tag, src, dst, cci, cco):
        A = arena(tag, [
            ("wq", [128, 8, 1024], BF16),
            ("wstage", [128, 8, 1024], F32),
            ("xg", [128, 2, TG, 1024], F32),
            ("xf", [128, 1024], F32),
            ("q1", [128, 1024], F32),
            ("q2", [128, 1024], F32),
            ("q3", [128, 1024], F32),
            ("msk", [128, 1024], U8),
            ("xb", [128, 1024], BF16),
            ("xT", [128, 8, 128], BF16),
            ("bias", [128, 1024], F32),
        ])
        P = psum(tag, [
            ("t", [128, 8, 128], BF16),
            ("w", [128, 8, 128], F32),
            ("y", [128, 1024], F32),
            ("m", [128, 128], F32),
        ])
        dW = p.dma("sync", A["wstage"][:, :, :],
                   WD["L1_ffn_weight"].ap().rearrange("(r p) c -> p r c", p=128), inc="sD")
        tAm = E("vector", lambda v: v.tensor_reduce(
            scal[:, 0:1], A["wstage"].rearrange("p a b -> p (a b)"), axis=AX,
            op=OP.max, apply_absolute_value=True), waits=[dW], inc="sV")
        tWm = cross_part_max(scal[:, 0:1], arow[0:1, 0:1], P["m"], [tAm])
        tWm2 = E("vector", lambda v: v.tensor_single_scalar(arow[0:1, 0:1], arow[0:1, 0:1],
                                                            1e-12, OP.max), waits=[tWm], inc="sV")
        tWb = bcast_scalar(scal[:, 1:2], arow[0:1, 0:1], P["m"][:, 0:1], [tWm2])
        tWr = E("vector", lambda v: v.reciprocal(scal[:, 2:3], scal[:, 1:2]), waits=[tWb],
                inc="sV")
        tWs = E("vector", lambda v: v.tensor_scalar(scal[:, 2:3], scal[:, 2:3], 448.0, None,
                                                    OP.mult), waits=[tWr], inc="sV")
        wq_tok = None
        for j in range(8):
            trs = None
            for r in range(8):
                trs = E("tensor", lambda t, j=j, r=r: t.transpose(
                    P["w"][:, r, :], A["wstage"][:, r, j * 128:(j + 1) * 128], ident_f[:, :]),
                    waits=T(tIf, dW, wq_tok), inc="sT")
            wq_tok = quant_chain(P["w"].rearrange("p a b -> p (a b)"), A["wq"][:, j, :],
                                 scal[:, 2:3], A["q1"], A["q2"], A["q3"], A["msk"],
                                 T(trs, tWs))
        dB = p.dma("sync", A["q1"][0:1, :], WD["L1_ffn_bias"].ap().rearrange("(a c) -> a c", a=1),
                   waits=[wq_tok], inc="sD")
        tb1 = E("tensor", lambda t: t.matmul(P["y"][:, 0:512], ones_row[:, :],
                                             A["q1"][0:1, 0:512], start=True, stop=True),
                waits=T(tOnes, dB), inc=None)
        tb2 = E("tensor", lambda t: t.matmul(P["y"][:, 512:1024], ones_row[:, :],
                                             A["q1"][0:1, 512:1024], start=True, stop=True),
                inc="sT", fence=True)
        tb3 = E("vector", lambda v: v.tensor_scalar(A["bias"][:, :], P["y"][:, :], 0.5, None,
                                                    OP.mult), waits=[tb2], inc="sV")
        xg_free = [None, None]
        am_tok = None
        for g in range(NG):
            xg = A["xg"][:, g % 2]
            dX = p.dma("sync", xg, x_re(src)[g], waits=T(xg_free[g % 2], tb3, wq_tok), inc="sD")
            for t in range(TG):
                t8 = ln_tile(xg[:, t, :], A["xf"], 8 * ((TG * g + t) % 8), T(dX, am_tok))
                tr = E("vector", lambda v: v.tensor_reduce(scal[:, 4:5], A["xf"][:, :], axis=AX,
                                                           op=OP.max, apply_absolute_value=True),
                       waits=[t8], inc="sV")
                if am_tok is None:
                    am_tok = E("vector", lambda v: v.tensor_copy(scal[:, 5:6], scal[:, 4:5]),
                               waits=[tr], inc="sV")
                else:
                    am_tok = E("vector", lambda v: v.tensor_tensor(scal[:, 5:6], scal[:, 5:6],
                                                                   scal[:, 4:5], OP.max),
                               waits=[tr], inc="sV")
            xg_free[g % 2] = am_tok
        tXm = cross_part_max(scal[:, 5:6], arow[0:1, 1:2], P["m"], [am_tok])
        tXm2 = E("vector", lambda v: v.tensor_single_scalar(arow[0:1, 1:2], arow[0:1, 1:2],
                                                            1e-12, OP.max), waits=[tXm], inc="sV")
        dC1 = p.dma("gpsimd", cci[:, :], arow[0:1, 1:2], waits=[tXm2], inc="sDg")
        tCC = E("gpsimd", lambda g_: g_.collective_compute(
            "AllReduce", OP.max, replica_groups=[list(range(8))],
            ins=[cci.ap()], outs=[cco.ap()]), waits=[dC1], inc="sC")
        dC2 = p.dma("scalar", arow[0:1, 2:3], cco[:, :], waits=[tCC], inc="sD")
        tXb = bcast_scalar(scal[:, 6:7], arow[0:1, 2:3], P["m"][:, 0:1], [dC2])
        tXr = E("vector", lambda v: v.reciprocal(scal[:, 7:8], scal[:, 6:7]), waits=[tXb],
                inc="sV")
        tXs = E("vector", lambda v: v.tensor_scalar(scal[:, 7:8], scal[:, 7:8], 448.0, None,
                                                    OP.mult), waits=[tXr], inc="sV")
        tE1 = E("vector", lambda v: v.tensor_tensor(scal[:, 8:9], scal[:, 6:7], scal[:, 1:2],
                                                    OP.mult), waits=[tXs, tWb], inc="sV")
        tE2 = E("vector", lambda v: v.tensor_scalar(scal[:, 8:9], scal[:, 8:9],
                                                    float(0.5 / (448.0 * 448.0)), None, OP.mult),
                waits=[tE1], inc="sV")
        dO = None
        mm_prev = None
        tq = None
        to1 = None
        te = None
        for g in range(NG):
            xg = A["xg"][:, g % 2]
            dX = p.dma("sync", xg, x_re(src)[g], waits=T(xg_free[g % 2], tE2), inc="sD")
            out_toks = []
            for t in range(TG):
                t8 = ln_tile(xg[:, t, :], A["xf"], 8 * ((TG * g + t) % 8), T(dX, tq))
                tq = quant_chain(A["xf"][:, :], A["xb"][:, :], scal[:, 7:8],
                                 A["q1"], A["q2"], A["q3"], A["msk"],
                                 T(t8, tXs, wq_tok, tb3, te))
                te = transposes_to(P["t"], A["xb"], A["xT"][:, :, :], [tq], extra=T(mm_prev))
                mm = None
                for cc in range(8):
                    for h in range(2):
                        last = (cc == 7 and h == 1)
                        mm = E("tensor", lambda tt, cc=cc, h=h: tt.matmul(
                            P["y"][:, h * 512:(h + 1) * 512], A["xT"][:, cc, :],
                            A["wq"][:, cc, h * 512:(h + 1) * 512],
                            start=(cc == 0), stop=(cc == 7)), waits=T(te, to1, tb3),
                            inc="sT" if last else None, fence=last)
                mm_prev = mm
                to1 = E("vector", lambda v, t=t, xg=xg: v.scalar_tensor_tensor(
                    xg[:, t, :], P["y"][:, :], scal[:, 8:9], xg[:, t, :], OP.mult, OP.add),
                    waits=T(mm, tE2, tb3), inc="sV")
                to2 = E("vector", lambda v, t=t, xg=xg: v.tensor_tensor(
                    xg[:, t, :], xg[:, t, :], A["bias"][:, :], OP.add),
                    waits=[to1], inc="sV")
                out_toks.append(to2)
            dO = p.dma("scalar", x_re(dst)[g], xg, waits=out_toks, inc="sDo")
            xg_free[g % 2] = dO
        return dO

    def phase_l1attn(tag, src, dst):
        A = arena(tag, [
            ("wqkv", [128, 8, 24, 128], BF16),
            ("wstage", [128, 1024], F32),
            ("wstage2", [128, 8, 3, 2, 64], F32),
            ("xg", [128, 2, TG, 1024], F32),
            ("x0", [128, 1024], F32),
            ("xb", [128, 1024], BF16),
            ("xT", [128, 8, 512], BF16),
            ("st", [128, 6, 512], BF16),
            ("vst2", [128, 2, 128], BF16),
            ("ktb", [128, 2, S], BF16),
            ("vtb", [128, 2, S // 128, 128], BF16),
            ("qtg", [128, 2, 8, 512], BF16),
            ("PT", [128, 3, 1024], BF16),
            ("oT", [128, 8, 512], BF16),
            ("rr", [128, 512], F32),
            ("wproj", [128, 8, 1024], BF16),
            ("bpr", [128, 1024], F32),
        ])
        P1 = psum(tag + "1", [
            ("t", [128, 8, 128], BF16),
            ("s2", [128, 2, 512], F32),
        ])
        psTv = P1["t"]
        P = psum(tag + "2", [
            ("s", [128, 2, 1024], F32),
            ("o", [128, 512], F32),
            ("n", [128, 512], F32),
            ("y", [128, 1024], F32),
        ])
        wq_toks = []
        dprev = None
        for cc in range(8):
            dW = None
            for k in range(3):
                for i in range(2):
                    dW = p.dma("sync", A["wstage2"][:, :, k, i, :],
                               bass.AP(WD["L1_attn_W_qkv"],
                                       cc * 128 * 3072 + 64 * k + 192 * i,
                                       [[3072, 128], [384, 8], [1, 64]]),
                               waits=T(dprev), inc="sD")
            dprev = E("vector", lambda v, cc=cc: v.tensor_copy(
                A["wqkv"][:, cc, :, :].rearrange("p a b -> p (a b)"),
                A["wstage2"].rearrange("p a b c d -> p (a b c d)")), waits=[dW], inc="sV")
            wq_toks.append(dprev)
        bq_toks = []
        for pr in range(8):
            for k in range(3):
                bq_toks.append(p.dma("sync", bqkv_col[:, pr * 3 + k:pr * 3 + k + 1],
                                     bass.AP(WD["L1_attn_b_qkv"], 384 * pr + 64 * k,
                                             [[192, 2], [1, 64]]), inc="sD"))
        WQ = A["wqkv"]
        xg_free = [None, None]
        st_free = [None] * 6
        ev_last = [None, None]
        vst2_free = [None, None]
        tr_prev = None
        qkv_dma = []
        for g in range(NG):
            xg = A["xg"][:, g % 2]
            dX = p.dma("sync", xg, x_re(src)[g], waits=T(xg_free[g % 2], *wq_toks[-2:]), inc="sD")
            tr_evac = []
            t9 = None
            te = None
            for t in range(TG):
                t9 = ln_tile(xg[:, t, :], A["x0"], 8 * ((TG * g + t) % 8), T(dX, t9, te),
                             bf_out=A["xb"])
                te = transposes_to(psTv, A["xb"], A["xT"][:, :, 128 * t:128 * (t + 1)],
                                   [t9], extra=T(tr_evac[-1] if tr_evac else None, tr_prev,
                                                 *wq_toks[-1:]))
                tr_evac.append(te)
            tr_prev = tr_evac[-1]
            for pr in range(8):
                for k in range(3):
                    b_ = (pr * 3 + k) % 2
                    mm = None
                    for cc in range(8):
                        mm = E("tensor", lambda t, pr=pr, k=k, cc=cc, b_=b_: t.matmul(
                            P1["s2"][:, b_, :], WQ[:, cc, pr * 3 + k, :],
                            A["xT"][:, cc, :], start=(cc == 0), stop=(cc == 7)),
                            waits=T(tr_evac[-1], wq_toks[-1], ev_last[b_]),
                            inc="sT" if cc == 7 else None, fence=(cc == 7))
                    bci = pr * 3 + k
                    si = k * 2 + (pr % 2)
                    ev = E("vector", lambda v, b_=b_, bci=bci, si=si: v.tensor_scalar(
                        A["st"][:, si, :], P1["s2"][:, b_, :], bqkv_col[:, bci:bci + 1],
                        None, OP.add), waits=T(mm, bq_toks[-1], st_free[si]), inc="sV")
                    ev_last[b_] = ev
                    if k == 0:
                        d_ = p.dma("scalar", bass.AP(qTd, pr * 128 * S + 512 * g,
                                                   [[S, 128], [1, 512]]),
                                   A["st"][:, si, :], waits=[ev], inc="sDo")
                        st_free[si] = d_
                    elif k == 1:
                        d_ = p.dma("scalar", bass.AP(ktd, pr * 128 * S + 512 * g,
                                                   [[S, 128], [1, 512]]),
                                   A["st"][:, si, :], waits=[ev], inc="sDo")
                        st_free[si] = d_
                    else:
                        tt = None
                        tv = None
                        for sx in range(TG):
                            tt = E("tensor", lambda t, sx=sx, si=si: t.transpose(
                                psTv[:, sx, :], A["st"][:, si, 128 * sx:128 * (sx + 1)],
                                ident_b[:, :]), waits=T(ev, tIb, tr_prev, tv), inc="sT",
                                fence=True)
                            tv = E("vector", lambda v, sx=sx: v.tensor_copy(
                                A["vst2"][:, sx % 2, :],
                                psTv[:, sx, :]), waits=T(tt, vst2_free[sx % 2]), inc="sV")
                            d_ = p.dma("scalar",
                                       bass.AP(vtd, pr * 128 * S + (4 * g + sx) * 128,
                                               [[S, 128], [1, 128]]),
                                       A["vst2"][:, sx % 2, :], waits=[tv], inc="sDo")
                            vst2_free[sx % 2] = d_
                        st_free[si] = tt
                        tr_prev = vst2_free[(TG - 1) % 2]
                    qkv_dma.append(st_free[si])
            xg_free[g % 2] = tr_evac[-1]
        d1_dma_done = ("sDo", p.counts["sDo"])
        wp_tok = None
        for j in range(8):
            dW = p.dma("sync", A["wstage"][:, :],
                       bass.AP(WD["L1_attn_W_proj"], j * 128 * 1024, [[1024, 128], [1, 1024]]),
                       waits=T(wp_tok), inc="sD")
            wp_tok = E("vector", lambda v, j=j: v.tensor_copy(A["wproj"][:, j, :],
                                                              A["wstage"][:, :]),
                       waits=[dW], inc="sV")
        dB = p.dma("sync", A["wstage"][0:1, :],
                   WD["L1_attn_b_proj"].ap().rearrange("(a c) -> a c", a=1), inc="sD")
        tb1 = E("tensor", lambda t: t.matmul(P["y"][:, 0:512], ones_row[:, :],
                                             A["wstage"][0:1, 0:512], start=True, stop=True),
                waits=T(tOnes, dB, tr_prev), inc=None)
        tb2 = E("tensor", lambda t: t.matmul(P["y"][:, 512:1024], ones_row[:, :],
                                             A["wstage"][0:1, 512:1024], start=True, stop=True),
                inc="sT", fence=True)
        tb3 = E("vector", lambda v: v.tensor_copy(A["bpr"][:, :], P["y"][:, :]),
                waits=[tb2], inc="sV")
        dO = None
        xg_free2 = None
        qtg_free = [None, None]
        ktb_free = [None, None]
        oT_free = None
        y_mm_prev = None
        exp_last = [None, None]
        pt_last = [None, None]
        y_mm_last_qg = None
        to1_qg_prev = None
        ptS_free = [d1_dma_done, d1_dma_done]
        for qg in range(NG):
            qb = qg % 2
            dQ = p.dma("sync", A["qtg"][:, qb, :, :],
                       bass.AP(qTd, 512 * qg, [[S, 128], [128 * S, 8], [1, 512]]),
                       waits=T(qtg_free[qb], d1_dma_done), inc="sD")
            dX = p.dma("sync", A["xg"][:, 0], x_re(src)[qg], waits=T(xg_free2, d1_dma_done),
                       inc="sD")
            pair_out = []
            for pr in range(8):
                kb = pr % 2
                dK = p.dma("sync", A["ktb"][:, kb, :],
                           bass.AP(ktd, pr * 128 * S, [[S, 128], [1, S]]),
                           waits=T(ktb_free[kb], d1_dma_done), inc="sD")
                dV = p.dma("sync", A["vtb"][:, kb, :, :].rearrange("p a b -> p (a b)"),
                           bass.AP(vtd, pr * 128 * S, [[S, 128], [1, S]]),
                           waits=T(ktb_free[kb]), inc="sD")
                NC_ = S // 128
                NP_ = NC_ // 2
                expD = [None] * NC_
                pvB_toks = [None] * NC_
                add_toks = [None] * NP_
                n_toks = [None] * NP_

                def PTb(k):
                    k = k % 4
                    if k == 3:
                        return A["st"][:, 4:6, :].rearrange("p a b -> p (a b)")
                    return A["PT"][:, k, :]

                def PTh(k, h):
                    k = k % 4
                    if k == 3:
                        return A["st"][:, 4 + h, :]
                    return A["PT"][:, k, 512 * h:512 * (h + 1)]

                def emit_n(j):
                    m = j % 2
                    E("tensor", lambda t, j=j, m=m: t.matmul(
                        P["n"][0:64, :], ones64[:, :], A["st"][:, 2 * m, :],
                        start=(j == 0), stop=(j == NP_ - 1),
                        tile_position=(0, 0), skip_group_check=True),
                        waits=T(tO64, add_toks[j]), inc=None)
                    n_toks[j] = E("tensor", lambda t, j=j, m=m: t.matmul(
                        P["n"][64:128, :], ones64[:, :], A["st"][:, 2 * m + 1, :],
                        start=(j == 0), stop=(j == NP_ - 1),
                        tile_position=(0, 64), skip_group_check=True),
                        inc="sT", fence=(j == NP_ - 1))

                for sc in range(NC_ + 1):
                    if sc < NC_:
                        par = sc % 2
                        E("tensor", lambda t, sc=sc, par=par, qb=qb, pr=pr, kb=kb: t.matmul(
                            P["s"][:, par, 0:512], A["ktb"][0:64, kb, 128 * sc:128 * (sc + 1)],
                            A["qtg"][0:64, qb, pr, :], start=True, stop=True,
                            tile_position=(0, 0)),
                            waits=T(dK, dQ, dV, expD[sc - 2] if sc >= 2 else None,
                                    y_mm_prev), inc=None)
                        m2 = E("tensor", lambda t, sc=sc, par=par, qb=qb, pr=pr, kb=kb: t.matmul(
                            P["s"][:, par, 512:1024],
                            A["ktb"][64:128, kb, 128 * sc:128 * (sc + 1)],
                            A["qtg"][64:128, qb, pr, :], start=True, stop=True,
                            tile_position=(64, 0), skip_group_check=True),
                            inc="sT", fence=True)
                        E("scalar", lambda s, sc=sc, par=par: s.activation(
                            PTb(sc), P["s"][:, par, :], AF.Exp),
                            waits=T(m2, pvB_toks[sc - 4] if sc >= 4 else None,
                                    add_toks[(sc - 4) // 2] if sc >= 4 else None), inc="sS")
                        expD[sc] = E("scalar", lambda s: s.drain(), inc="sS")
                        if par == 1:
                            j = sc // 2
                            add_toks[j] = E("vector", lambda v, j=j: v.tensor_tensor(
                                A["st"][:, 2 * (j % 2):2 * (j % 2) + 2, :].rearrange(
                                    "p a b -> p (a b)"),
                                PTb(2 * j), PTb(2 * j + 1), OP.add),
                                waits=T(expD[sc],
                                        n_toks[j - 2] if j >= 2 else ptS_free[j % 2]),
                                inc="sV")
                    if sc >= 1:
                        sp = sc - 1
                        E("tensor", lambda t, sp=sp, pr=pr, kb=kb: t.matmul(
                            P["o"][0:64, :], A["vtb"][:, kb, sp, 0:64],
                            PTh(sp, 0), start=(sp == 0),
                            stop=(sp == NC_ - 1), tile_position=(0, 0)),
                            waits=T(expD[sp], dV, oT_free), inc=None)
                        pvB_toks[sp] = E("tensor", lambda t, sp=sp, pr=pr, kb=kb: t.matmul(
                            P["o"][64:128, :], A["vtb"][:, kb, sp, 64:128],
                            PTh(sp, 1), start=(sp == 0),
                            stop=(sp == NC_ - 1), tile_position=(0, 64),
                            skip_group_check=True), inc="sT")
                        if sp % 2 == 1 and sp >= 3:
                            emit_n(sp // 2 - 1)
                emit_n(NP_ - 1)
                ptS_free = [n_toks[NP_ - 2], n_toks[NP_ - 1]]
                ktb_free[kb] = n_toks[NP_ - 1]
                trr = E("vector", lambda v: v.reciprocal(A["rr"][:, :], P["n"][:, :]),
                        waits=[n_toks[NP_ - 1]], inc="sV")
                tev = E("vector", lambda v, pr=pr: v.tensor_tensor(
                    A["oT"][:, pr, :], P["o"][:, :], A["rr"][:, :], OP.mult),
                    waits=T(trr, y_mm_last_qg), inc="sV")
                oT_free = tev
                pair_out.append(tev)
            qtg_free[qb] = pair_out[-1]
            out_toks = []
            to1 = None
            for t in range(TG):
                mm = None
                for pr in range(8):
                    for h in range(2):
                        last = (pr == 7 and h == 1)
                        mm = E("tensor", lambda tt, t=t, pr=pr, h=h: tt.matmul(
                            P["y"][:, h * 512:(h + 1) * 512],
                            A["oT"][:, pr, 128 * t:128 * (t + 1)],
                            A["wproj"][:, pr, h * 512:(h + 1) * 512],
                            start=(pr == 0), stop=(pr == 7)),
                            waits=T(pair_out[-1], wp_tok, to1, to1_qg_prev, tb3),
                            inc="sT" if last else None, fence=last)
                y_mm_prev = mm
                y_mm_last_qg = mm
                to1 = E("vector", lambda v, t=t: v.tensor_tensor(
                    A["xg"][:, 0, t, :], A["xg"][:, 0, t, :], P["y"][:, :], OP.add),
                    waits=T(mm, dX, tb3), inc="sV")
                to2 = E("vector", lambda v, t=t: v.tensor_tensor(
                    A["xg"][:, 0, t, :], A["xg"][:, 0, t, :], A["bpr"][:, :], OP.add),
                    waits=[to1], inc="sV")
                out_toks.append(to2)
            to1_qg_prev = to1
            dO = p.dma("scalar", x_re(dst)[qg], A["xg"][:, 0], waits=out_toks, inc="sDo")
            xg_free2 = dO
        return dO

    chain_src = [x_in, xs[0], xs[1], xs[2], xs[3], xs[4]]
    chain_dst = [xs[0], xs[1], xs[2], xs[3], xs[4], y_out]
    if n_sub < 6:
        chain_dst[n_sub - 1] = y_out
    phases = [
        lambda i: phase_l0ffn("A", chain_src[0], chain_dst[0]),
        lambda i: phase_l0attn("B", chain_src[1], chain_dst[1]),
        lambda i: phase_l0ffn("A2", chain_src[2], chain_dst[2]),
        lambda i: phase_l1ffn("CC", chain_src[3], chain_dst[3], *ccio[0]),
        lambda i: phase_l1attn("D", chain_src[4], chain_dst[4]),
        lambda i: phase_l1ffn("EE", chain_src[5], chain_dst[5], *ccio[1]),
    ]
    for i in range(n_sub):
        phases[i](i)
        p.barrier()
    E("sync", lambda e: e.nop(), waits=[("sDo", p.counts["sDo"]), ("sDg", p.counts["sDg"])])

    sems = {n_: nc.alloc_semaphore(n_) for n_ in
            ("sD", "sDo", "sDg", "sV", "sS", "sT", "sG", "sC")}
    with nc.Block() as block:
        p.build_into(block, sems)
    return nc


def kernel(**inputs):
    x_all = np.ascontiguousarray(inputs["L0_attn_x"], dtype=np.float32)
    weights = {k: np.ascontiguousarray(v, dtype=np.float32) for k, v in inputs.items()
               if k != "L0_attn_x"}
    nc = build_program()
    in_maps = [dict(x=x_all[i], **weights) for i in range(8)]
    res = run_bass_kernel_spmd(nc, in_maps, core_ids=list(range(8)))
    return np.stack([np.asarray(res.results[i]["y"], dtype=np.float32) for i in range(8)],
                    axis=0)


# version 12
# speedup vs baseline: 1.0773x; 1.0415x over previous
import numpy as np
import concourse.bass as bass
import concourse.mybir as mybir
from concourse.alu_op_type import AluOpType as OP
from concourse.bass_utils import run_bass_kernel_spmd

F32 = mybir.dt.float32
BF16 = mybir.dt.bfloat16
U8 = mybir.dt.uint8
AF = mybir.ActivationFunctionType
AX = mybir.AxisListType.X

S = 4096
C = 1024
NG = 8
TG = 4
EPS = 1e-5
ENGS = ("tensor", "vector", "scalar", "gpsimd", "sync")


class Prog:

    def __init__(self):
        self.counts = {}
        self.ops = {e: [] for e in ENGS}
        self.lastw = {e: {} for e in ENGS}
        for s_ in ("sD", "sDo", "sDg", "sV", "sS", "sT", "sG", "sC"):
            self.counts[s_] = 0

    def emit(self, eng, build, waits=(), inc=None, n=1, fence=False):
        w = []
        for t in waits:
            if t is None:
                continue
            s, v = t
            if self.lastw[eng].get(s, 0) >= v:
                continue
            self.lastw[eng][s] = v
            w.append((s, v))
        tok = None
        if inc is not None:
            self.counts[inc] += n
            tok = (inc, self.counts[inc])
        self.ops[eng].append((w, build, inc, n, fence))
        return tok

    def dma(self, eng, out, in_, waits=(), inc="sD"):
        return self.emit(eng, lambda e: e.dma_start(out, in_), waits, inc, 16)

    def barrier(self):
        toks = [(s, v) for s, v in self.counts.items() if v > 0]
        for eng in ENGS:
            self.emit(eng, lambda e: e.nop(), waits=toks)

    def build_into(self, block, sems):
        for eng in ENGS:
            ops = self.ops[eng]
            if not ops:
                continue

            def body(e, ops=ops, eng=eng):
                for waits, build, inc, n, fence in ops:
                    for s, v in waits:
                        e.wait_ge(sems[s], v)
                    ins = build(e)
                    if inc is not None:
                        if fence:
                            e.drain().then_inc(sems[inc], n)
                        else:
                            ins.then_inc(sems[inc], n)

            getattr(block, eng)(body)


def T(*toks):
    return [t for t in toks if t is not None]


def build_program(n_sub=6):
    nc = bass.Bass("TRN2", target_bir_lowering=False, detect_race_conditions=False)

    x_in = nc.dram_tensor("x", [S, C], F32, kind="ExternalInput")
    y_out = nc.dram_tensor("y", [S, C], F32, kind="ExternalOutput")
    WD = {}
    for nm, shp in [
        ("L0_attn_W_q", [C, C]), ("L0_attn_W_k", [C, C]), ("L0_attn_W_v", [C, C]),
        ("L0_attn_W_o", [C, C]), ("L0_ffn_W_r", [C, C]), ("L0_ffn_W_k", [C, C]),
        ("L0_ffn_W_v", [C, C]), ("L0_ffn_mu_r", [C]), ("L0_ffn_mu_k", [C]),
        ("L1_attn_W_qkv", [C, 3 * C]), ("L1_attn_b_qkv", [3 * C]),
        ("L1_attn_W_proj", [C, C]), ("L1_attn_b_proj", [C]),
        ("L1_ffn_weight", [C, C]), ("L1_ffn_bias", [C]),
    ]:
        WD[nm] = nc.dram_tensor(nm, shp, F32, kind="ExternalInput")
    xs = [nc.dram_tensor(f"xs{i}", [S, C], F32) for i in range(1, 6)]
    qTd = nc.dram_tensor("qTd", [8, 128, S], BF16)
    ktd = nc.dram_tensor("ktd", [8, 128, S], BF16)
    vtd = nc.dram_tensor("vtd", [8, 128, S // 128, 128], BF16)
    ccio = [(nc.dram_tensor(f"cci{i}", [1, 1], F32), nc.dram_tensor(f"cco{i}", [1, 1], F32))
            for i in range(2)]

    p = Prog()
    E = p.emit

    sb = nc.alloc_sbuf_tensor
    ident_b = sb("ident_b", [128, 128], BF16)
    ident_f = sb("ident_f", [128, 128], F32)
    i0 = sb("i0", [128, 128], F32)
    ones_row = sb("ones_row", [1, 128], F32)
    ones64 = sb("ones64", [128, 64], BF16)
    cols = sb("cols", [128, 64], F32)
    lnjunk = sb("lnjunk", [128, C], F32)
    mu_sb = sb("mu_sb", [128, 16], F32)
    bqkv_col = sb("bqkv_col", [128, 24], F32)
    scal = sb("scal", [128, 16], F32)
    arow = sb("arow", [1, 8], F32)
    PERSIST_BASE = nc.sbuf_base
    SBUF_TOP = nc.sbuf_top

    E("gpsimd", lambda g: g.iota(i0[:, :], [[1, 128]], channel_multiplier=0,
                                 allow_small_or_imprecise_dtypes=True))
    tIo = E("gpsimd", lambda g: g.iota(ident_f[:, :], [[0, 128]], channel_multiplier=1,
                                       allow_small_or_imprecise_dtypes=True), inc="sG")
    tIb = E("vector", lambda v: v.tensor_tensor(ident_b[:, :], i0[:, :], ident_f[:, :],
                                                OP.is_equal), waits=[tIo], inc="sV")
    tIf = E("vector", lambda v: v.tensor_tensor(ident_f[:, :], i0[:, :], ident_f[:, :],
                                                OP.is_equal), waits=[tIb], inc="sV")
    tOnes = E("gpsimd", lambda g: g.memset(ones_row[:, :], 1.0), inc="sG")
    tO64 = E("gpsimd", lambda g: g.memset(ones64[:, :], 1.0), inc="sG")

    def arena(tag, items):
        nc.sbuf_base = PERSIST_BASE
        out = {}
        for nm, shp, dt in items:
            out[nm] = nc.alloc_sbuf_tensor(tag + nm, shp, dt).ap()
        assert nc.sbuf_base <= SBUF_TOP, f"SBUF overflow {tag}: {nc.sbuf_base}>{SBUF_TOP}"
        return out

    def psum(tag, items):
        nc.psum_base = 0
        out = {}
        for nm, shp, dt in items:
            h = nc.alloc_psum_tensor(tag + nm, shp, dt)
            out[nm] = h.ap()
            out[nm + "_h"] = h
        assert nc.psum_base <= 8
        return out

    def ln_tile(xg_tile, x0_out, ci, tok_in, bf_out=None):
        c8 = cols[:, ci:ci + 8]
        t1 = E("vector", lambda v: v.reduce_sum(c8[:, 0:1], xg_tile, axis=AX),
               waits=tok_in, inc="sV")
        t2 = E("vector", lambda v: v.tensor_scalar(c8[:, 1:2], c8[:, 0:1], -1.0 / C, None,
                                                   OP.mult), waits=[t1], inc="sV")
        t3 = E("scalar", lambda s: s.activation(x0_out, xg_tile, AF.Identity, bias=c8[:, 1:2],
                                                scale=1.0), waits=T(t2, *tok_in), inc="sS")
        t4 = E("scalar", lambda s: s.activation(lnjunk[:, :], xg_tile, AF.Square,
                                                bias=c8[:, 1:2], scale=1.0,
                                                accum_out=c8[:, 2:3]), inc="sS")
        t5 = E("vector", lambda v: v.tensor_scalar(c8[:, 3:4], c8[:, 2:3], 1.0 / C, EPS,
                                                   OP.mult, op1=OP.add), waits=[t4], inc="sV")
        t6 = E("scalar", lambda s: s.activation(c8[:, 4:5], c8[:, 3:4], AF.Sqrt),
               waits=[t5], inc="sS")
        t7 = E("vector", lambda v: v.reciprocal(c8[:, 5:6], c8[:, 4:5]), waits=[t6], inc="sV")
        t8 = E("vector", lambda v: v.tensor_scalar(x0_out, x0_out, c8[:, 5:6], None, OP.mult),
               waits=[t7, t3], inc="sV")
        if bf_out is None:
            return t8
        return E("vector", lambda v: v.tensor_copy(bf_out, x0_out), waits=[t8], inc="sV")

    def transposes_to(psT, src_bf, dst_sb, tok_in, extra=()):
        trs = None
        for j in range(8):
            trs = E("tensor", lambda t, j=j: t.transpose(psT[:, j, :],
                                                         src_bf[:, j * 128:(j + 1) * 128],
                                                         ident_b[:, :]),
                    waits=T(tIb, *tok_in, *extra), inc="sT" if j == 7 else None, fence=(j == 7))
        return E("vector", lambda v: v.tensor_copy(dst_sb, psT[:, :, :]), waits=[trs], inc="sV")

    def cross_part_max(col1, out11, ps_m, tok_in):
        tt = E("tensor", lambda t: t.transpose(ps_m[0:1, 0:128], col1, ident_f[:, :]),
               waits=T(tIf, *tok_in), inc="sT", fence=True)
        return E("vector", lambda v: v.reduce_max(out11, ps_m[0:1, 0:128], axis=AX),
                 waits=[tt], inc="sV")

    def bcast_scalar(dst_col, src11, ps_col, tok_in):
        tm = E("tensor", lambda t: t.matmul(ps_col, ones_row[:, :], src11, start=True,
                                            stop=True), waits=T(tOnes, *tok_in), inc="sT",
               fence=True)
        return E("vector", lambda v: v.tensor_copy(dst_col, ps_col), waits=[tm], inc="sV")

    def prep_wT(wdram, dst, wstage, psW, scale, extra=()):
        dW = p.dma("sync", wstage[:, :, :], wdram.ap().rearrange("(r p) c -> p r c", p=128),
                   waits=list(extra), inc="sD")
        tprev = None
        for j in range(8):
            trs = None
            for r in range(8):
                trs = E("tensor", lambda t, j=j, r=r: t.transpose(
                    psW[:, r, :], wstage[:, r, j * 128:(j + 1) * 128], ident_f[:, :]),
                    waits=T(tIf, dW, tprev), inc="sT" if r == 7 else None, fence=(r == 7))
            tprev = E("vector", lambda v, j=j: v.tensor_scalar(dst[:, j, :], psW[:, :, :],
                                                               scale, None, OP.mult),
                      waits=[trs], inc="sV")
        return tprev, dW

    def quant_chain(src_ap, dst_bf, scol, q1, q2, q3, msk, tok_in):
        C20 = float(2 ** 20 + 1)
        MM = float(1.5 * 2 ** 14)
        E("vector", lambda v: v.tensor_scalar(q1, src_ap, scol, None, OP.mult), waits=tok_in)
        E("vector", lambda v: v.tensor_scalar(q2, q1, C20, None, OP.mult))
        E("vector", lambda v: v.tensor_tensor(q3, q2, q1, OP.subtract))
        E("vector", lambda v: v.tensor_tensor(q2, q2, q3, OP.subtract))
        E("vector", lambda v: v.tensor_tensor(q3, q1, q1, OP.mult))
        E("vector", lambda v: v.tensor_single_scalar(msk, q3, float(2.0 ** -12), OP.is_ge))
        E("vector", lambda v: v.tensor_scalar(q3, q1, MM, None, OP.add))
        E("vector", lambda v: v.tensor_scalar(q3, q3, MM, None, OP.subtract))
        return E("vector", lambda v: v.select(dst_bf, msk, q2, q3), inc="sV")

    x_re = lambda d: d.ap().rearrange("(g t p) c -> g p t c", p=128, t=TG)

    def phase_l0ffn(tag, src, dst):
        A = arena(tag, [
            ("wt", [128, 8, 3072], BF16),
            ("wstage", [128, 8, 1024], F32),
            ("xg", [128, 2, TG, 1024], F32),
            ("x0", [128, 2, 1024], F32),
            ("xb", [128, 2, 1024], BF16),
            ("xT", [128, 8, 513], BF16),
            ("dmix", [128, 512], BF16),
            ("xrT", [128, 8, 512], BF16),
            ("xkT", [128, 8, 512], BF16),
            ("kact", [128, 8, 512], BF16),
            ("relu", [128, 512], F32),
            ("rsb", [128, 2, 1024], F32),
            ("tmp", [128, 2, 1024], F32),
        ])
        P = psum(tag, [
            ("t", [128, 8, 128], BF16),
            ("k", [128, 2, 512], F32),
            ("r", [128, 1024], F32),
            ("v", [128, 1024], F32),
        ])
        psW = P["r"].rearrange("p (a b) -> p a b", a=8)
        wt = A["wt"]
        tw0, _ = prep_wT(WD["L0_ffn_W_r"], wt[:, :, 0:1024], A["wstage"], psW, 1.0)
        tw1, _ = prep_wT(WD["L0_ffn_W_k"], wt[:, :, 1024:2048], A["wstage"], psW, 1.0, [tw0])
        tw2, _ = prep_wT(WD["L0_ffn_W_v"], wt[:, :, 2048:3072], A["wstage"], psW, 0.5, [tw1])
        dmu = p.emit("gpsimd", lambda g: g.dma_start(
            mu_sb[:, 0:8], bass.AP(WD["L0_ffn_mu_r"], 0, [[1, 128], [128, 8]]),
            allow_slow_non_contiguous=True), inc="sDg", n=16)
        dmu2 = p.emit("gpsimd", lambda g: g.dma_start(
            mu_sb[:, 8:16], bass.AP(WD["L0_ffn_mu_k"], 0, [[1, 128], [128, 8]]),
            allow_slow_non_contiguous=True), inc="sDg", n=16)
        tb0 = E("vector", lambda v: v.memset(A["xT"][:, :, 0:1], 0.0), inc="sV")
        wtoks = [tw0, tw1, tw2, dmu, dmu2, tb0]

        xg_free = [None, None]
        bound_tok = tb0
        mix_last = None
        v_mm_last = None
        r_mm_last = None
        tsig_prev = dmu
        tm1_hist = [None, None]
        tm1_prev = None
        t9h = [None, None]
        teh = [None, None]
        dO = None
        for g in range(NG):
            xg = A["xg"][:, g % 2]
            dX = p.dma("sync", xg, x_re(src)[g], waits=T(xg_free[g % 2], *wtoks), inc="sD")
            if g > 0:
                bound_tok = E("vector", lambda v: v.tensor_copy(A["xT"][:, :, 0:1],
                                                                A["xT"][:, :, 512:513]),
                              waits=T(mix_last), inc="sV")
            tr_evac = []
            for t in range(TG):
                t9 = ln_tile(xg[:, t, :], A["x0"][:, t % 2, :], 8 * ((TG * g + t) % 8),
                             T(dX, t9h[t % 2], teh[t % 2]), bf_out=A["xb"][:, t % 2, :])
                t9h[t % 2] = t9
                te = transposes_to(P["t"], A["xb"][:, t % 2, :], A["xT"].rearrange(
                    "p c m -> p c m")[:, :, 1 + 128 * t:1 + 128 * (t + 1)],
                    [t9], extra=T(tr_evac[-1] if tr_evac else None, bound_tok, mix_last))
                teh[t % 2] = te
                tr_evac.append(te)
            mix_toks = []
            for cc in range(8):
                td = E("vector", lambda v, cc=cc: v.tensor_tensor(
                    A["dmix"], A["xT"][:, cc, 1:513], A["xT"][:, cc, 0:513 - 1], OP.subtract),
                    waits=T(tr_evac[-1], bound_tok, dmu, dmu2,
                            r_mm_last), inc="sV")
                tr_ = E("vector", lambda v, cc=cc: v.scalar_tensor_tensor(
                    A["xrT"][:, cc, :], A["dmix"], mu_sb[:, cc:cc + 1], A["xT"][:, cc, 0:512],
                    OP.mult, OP.add), waits=[td], inc="sV")
                tk_ = E("vector", lambda v, cc=cc: v.scalar_tensor_tensor(
                    A["xkT"][:, cc, :], A["dmix"], mu_sb[:, 8 + cc:9 + cc], A["xT"][:, cc, 0:512],
                    OP.mult, OP.add), inc="sV")
                mix_toks += [tr_, tk_]
            mix_last = mix_toks[-1]
            kact_toks = []
            for dch in range(8):
                mm = None
                for cc in range(8):
                    mm = E("tensor", lambda t, dch=dch, cc=cc: t.matmul(
                        P["k"][:, dch % 2, :], wt[:, cc, 1024 + dch * 128:1024 + (dch + 1) * 128],
                        A["xkT"][:, cc, :], start=(cc == 0), stop=(cc == 7)),
                        waits=T(mix_last, kact_toks[-2] if len(kact_toks) >= 2 else None,
                                v_mm_last), inc="sT" if cc == 7 else None, fence=(cc == 7))
                t1 = E("vector", lambda v, dch=dch: v.tensor_single_scalar(
                    A["relu"], P["k"][:, dch % 2, :], 0.0, OP.max), waits=[mm], inc="sV")
                t2 = E("vector", lambda v, dch=dch: v.tensor_tensor(
                    A["kact"][:, dch, :], A["relu"], A["relu"], OP.mult), waits=[t1], inc="sV")
                kact_toks.append(t2)
            out_toks = []
            for t in range(TG):
                mmr = None
                for cc in range(8):
                    for h in range(2):
                        last = (cc == 7 and h == 1)
                        mmr = E("tensor", lambda tt, t=t, cc=cc, h=h: tt.matmul(
                            P["r"][:, h * 512:(h + 1) * 512],
                            A["xrT"][:, cc, 128 * t:128 * (t + 1)],
                            wt[:, cc, h * 512:(h + 1) * 512],
                            start=(cc == 0), stop=(cc == 7)),
                            waits=T(tsig_prev), inc="sT" if last else None, fence=last)
                r_mm_last = mmr
                tsig = E("scalar", lambda s, t=t: s.activation(A["rsb"][:, t % 2, :], P["r"][:, :],
                                                               AF.Sigmoid),
                         waits=T(mmr, tm1_hist[t % 2]), inc="sS")
                tsig_prev = tsig
                mmv = None
                for dch in range(8):
                    for h in range(2):
                        last = (dch == 7 and h == 1)
                        mmv = E("tensor", lambda tt, t=t, dch=dch, h=h: tt.matmul(
                            P["v"][:, h * 512:(h + 1) * 512],
                            A["kact"][:, dch, 128 * t:128 * (t + 1)],
                            wt[:, dch, 2048 + h * 512:2048 + (h + 1) * 512],
                            start=(dch == 0), stop=(dch == 7)),
                            waits=T(kact_toks[-1], tm1_prev),
                            inc="sT" if last else None, fence=last)
                v_mm_last = mmv
                tm1 = E("vector", lambda v, t=t: v.tensor_tensor(
                    A["tmp"][:, t % 2, :], A["rsb"][:, t % 2, :], P["v"][:, :], OP.mult),
                    waits=T(mmv, tsig), inc="sV")
                tm1_hist[t % 2] = tm1
                tm1_prev = tm1
                tm2 = E("vector", lambda v, t=t, xg=xg: v.tensor_tensor(
                    xg[:, t, :], xg[:, t, :], A["tmp"][:, t % 2, :], OP.add),
                    waits=[tm1], inc="sV")
                out_toks.append(tm2)
            dO = p.dma("scalar", x_re(dst)[g], xg, waits=out_toks, inc="sDo")
            xg_free[g % 2] = dO
        return dO

    def phase_l0attn(tag, src, dst):
        A = arena(tag, [
            ("wt", [128, 8, 4096], BF16),
            ("wstage", [128, 8, 1024], F32),
            ("xg", [128, 2, TG, 1024], F32),
            ("x0", [128, 2, 1024], F32),
            ("xb", [128, 2, 1024], BF16),
            ("xT", [128, 8, 512], BF16),
            ("ksb", [128, TG, 1024], BF16),
            ("vsb", [128, TG, 1024], BF16),
            ("kvb", [128, 8, 128], BF16),
            ("aT", [128, 8, 512], BF16),
            ("qst", [128, 2, 512], BF16),
            ("qg", [128, 8, 512], BF16),
        ])
        P = psum(tag, [
            ("t", [128, 8, 128], BF16),
            ("q", [128, 2, 512], F32),
            ("k", [128, 1024], F32),
            ("v", [128, 1024], F32),
            ("kv", [128, 8, 64], F32),
        ])
        wt = A["wt"]
        tw0, _ = prep_wT(WD["L0_attn_W_q"], wt[:, :, 0:1024], A["wstage"], P["k"].rearrange(
            "p (a b) -> p a b", a=8), 0.125)
        tw1, _ = prep_wT(WD["L0_attn_W_k"], wt[:, :, 1024:2048], A["wstage"],
                         P["k"].rearrange("p (a b) -> p a b", a=8), 1.0, [tw0])
        tw2, _ = prep_wT(WD["L0_attn_W_v"], wt[:, :, 2048:3072], A["wstage"],
                         P["k"].rearrange("p (a b) -> p a b", a=8), 1.0, [tw1])
        tw3, _ = prep_wT(WD["L0_attn_W_o"], wt[:, :, 3072:4096], A["wstage"],
                         P["k"].rearrange("p (a b) -> p a b", a=8), 1.0, [tw2])
        wtoks = [tw0, tw1, tw2, tw3]
        xg_free = [None, None]
        kv_last = None
        q_dma = [None, None]
        qev_hist = [None, None]
        ksb_free = None
        t9h = [None, None]
        teh = [None, None]
        for g in range(NG):
            xg = A["xg"][:, g % 2]
            dX = p.dma("sync", xg, x_re(src)[g], waits=T(xg_free[g % 2], *wtoks), inc="sD")
            tr_evac = []
            for t in range(TG):
                t9 = ln_tile(xg[:, t, :], A["x0"][:, t % 2, :], 8 * ((TG * g + t) % 8),
                             T(dX, t9h[t % 2], teh[t % 2]), bf_out=A["xb"][:, t % 2, :])
                t9h[t % 2] = t9
                te = transposes_to(P["t"], A["xb"][:, t % 2, :],
                                   A["xT"][:, :, 128 * t:128 * (t + 1)],
                                   [t9], extra=T(tr_evac[-1] if tr_evac else None, *wtoks))
                teh[t % 2] = te
                tr_evac.append(te)
            q_toks = []
            for pr in range(8):
                mm = None
                for cc in range(8):
                    mm = E("tensor", lambda t, pr=pr, cc=cc: t.matmul(
                        P["q"][:, pr % 2, :], wt[:, cc, pr * 128:(pr + 1) * 128],
                        A["xT"][:, cc, :], start=(cc == 0), stop=(cc == 7)),
                        waits=T(tr_evac[-1], qev_hist[pr % 2]),
                        inc="sT" if cc == 7 else None, fence=(cc == 7))
                ev = E("vector", lambda v, pr=pr: v.tensor_copy(A["qst"][:, pr % 2, :],
                                                                P["q"][:, pr % 2, :]),
                       waits=T(mm, q_dma[pr % 2]), inc="sV")
                dq = p.dma("scalar", bass.AP(qTd, pr * 128 * S + 512 * g, [[S, 128], [1, 512]]),
                           A["qst"][:, pr % 2, :], waits=[ev], inc="sDo")
                q_dma[pr % 2] = dq
                qev_hist[pr % 2] = ev
                q_toks.append(ev)
            kv_sb_toks = []
            for t in range(TG):
                mmk = None
                for cc in range(8):
                    for h in range(2):
                        last = (cc == 7 and h == 1)
                        mmk = E("tensor", lambda tt, t=t, cc=cc, h=h: tt.matmul(
                            P["k"][:, h * 512:(h + 1) * 512],
                            A["xT"][:, cc, 128 * t:128 * (t + 1)],
                            wt[:, cc, 1024 + h * 512:1024 + (h + 1) * 512],
                            start=(cc == 0), stop=(cc == 7)),
                            waits=T(ksb_free), inc="sT" if last else None, fence=last)
                tk = E("vector", lambda v, t=t: v.tensor_copy(A["ksb"][:, t, :], P["k"][:, :]),
                       waits=[mmk], inc="sV")
                mmv = None
                for cc in range(8):
                    for h in range(2):
                        last = (cc == 7 and h == 1)
                        mmv = E("tensor", lambda tt, t=t, cc=cc, h=h: tt.matmul(
                            P["v"][:, h * 512:(h + 1) * 512],
                            A["xT"][:, cc, 128 * t:128 * (t + 1)],
                            wt[:, cc, 2048 + h * 512:2048 + (h + 1) * 512],
                            start=(cc == 0), stop=(cc == 7)),
                            inc="sT" if last else None, fence=last)
                tv = E("vector", lambda v, t=t: v.tensor_copy(A["vsb"][:, t, :], P["v"][:, :]),
                       waits=[mmv], inc="sV")
                kv_sb_toks += [tk, tv]
            for t in range(TG):
                for pr in range(8):
                    first = (g == 0 and t == 0 and pr == 0)
                    last = (g == NG - 1 and t == TG - 1 and pr == 7)
                    glast = (t == TG - 1 and pr == 7)
                    E("tensor", lambda tt, t=t, pr=pr, first=first, last=last: tt.matmul(
                        P["kv"][0:64, pr, :], A["ksb"][:, t, 128 * pr:128 * pr + 64],
                        A["vsb"][:, t, 128 * pr:128 * pr + 64],
                        start=first, stop=last, skip_group_check=True,
                        tile_position=(0, 0)), waits=T(kv_sb_toks[-1]), inc=None)
                    kv_last = E("tensor", lambda tt, t=t, pr=pr, first=first, last=last: tt.matmul(
                        P["kv"][64:128, pr, :], A["ksb"][:, t, 128 * pr + 64:128 * (pr + 1)],
                        A["vsb"][:, t, 128 * pr + 64:128 * (pr + 1)],
                        start=first, stop=last, skip_group_check=True,
                        tile_position=(0, 64)), inc="sT" if glast else None, fence=last)
            ksb_free = kv_last
            xg_free[g % 2] = tr_evac[-1]
        tz = E("vector", lambda v: v.memset(A["kvb"][:, :, :], 0.0), inc="sV")
        kv_ev = tz
        for pr in range(8):
            E("vector", lambda v, pr=pr: v.tensor_copy(A["kvb"][0:64, pr, 0:64],
                                                       P["kv"][0:64, pr, :]),
              waits=T(kv_last, tz))
            kv_ev = E("vector", lambda v, pr=pr: v.tensor_copy(A["kvb"][64:128, pr, 64:128],
                                                               P["kv"][64:128, pr, :]), inc="sV")
        psA2 = P["k"]
        psY = P["v"]
        qg_dma_free = None
        aev_hist = [None, None]
        to_prev = None
        dO = None
        for g in range(NG):
            xg = A["xg"][:, g % 2]
            dX = p.dma("sync", xg, x_re(src)[g], waits=T(xg_free[g % 2], kv_ev), inc="sD")
            dq = p.dma("sync", A["qg"][:, :, :],
                       bass.AP(qTd, 512 * g, [[S, 128], [128 * S, 8], [1, 512]]),
                       waits=T(qg_dma_free, kv_ev, q_dma[0], q_dma[1]), inc="sD")
            a_toks = []
            for pr in range(8):
                b_ = pr % 2
                ma = E("tensor", lambda t, pr=pr, b_=b_: t.matmul(
                    psA2[0:64, b_ * 512:(b_ + 1) * 512], A["kvb"][:, pr, 0:64],
                    A["qg"][:, pr, :], start=True, stop=True, tile_position=(0, 0)),
                    waits=T(kv_ev, dq, aev_hist[b_]), inc=None)
                mb = E("tensor", lambda t, pr=pr, b_=b_: t.matmul(
                    psA2[64:128, b_ * 512:(b_ + 1) * 512], A["kvb"][:, pr, 64:128],
                    A["qg"][:, pr, :], start=True, stop=True, tile_position=(0, 64),
                    skip_group_check=True), inc="sT", fence=True)
                te = E("vector", lambda v, pr=pr, b_=b_: v.tensor_copy(
                    A["aT"][:, pr, :], psA2[:, b_ * 512:(b_ + 1) * 512]),
                    waits=[mb], inc="sV")
                aev_hist[b_] = te
                a_toks.append(te)
            qg_dma_free = a_toks[-1]
            out_toks = []
            for t in range(TG):
                mm = None
                for pr in range(8):
                    for h in range(2):
                        last = (pr == 7 and h == 1)
                        mm = E("tensor", lambda tt, t=t, pr=pr, h=h: tt.matmul(
                            psY[:, h * 512:(h + 1) * 512], A["aT"][:, pr, 128 * t:128 * (t + 1)],
                            wt[:, pr, 3072 + h * 512:3072 + (h + 1) * 512],
                            start=(pr == 0), stop=(pr == 7)),
                            waits=T(a_toks[-1], to_prev), inc="sT" if last else None,
                            fence=last)
                to = E("vector", lambda v, t=t, xg=xg: v.tensor_tensor(
                    xg[:, t, :], xg[:, t, :], psY[:, :], OP.add),
                    waits=T(mm, dX), inc="sV")
                to_prev = to
                out_toks.append(to)
            dO = p.dma("scalar", x_re(dst)[g], xg, waits=out_toks, inc="sDo")
            xg_free[g % 2] = dO
        return dO

    def phase_l1ffn(tag, src, dst, cci, cco):
        A = arena(tag, [
            ("wq", [128, 8, 1024], BF16),
            ("wstage", [128, 8, 1024], F32),
            ("xg", [128, 2, TG, 1024], F32),
            ("xf", [128, 1024], F32),
            ("q1", [128, 1024], F32),
            ("q2", [128, 1024], F32),
            ("q3", [128, 1024], F32),
            ("msk", [128, 1024], U8),
            ("xb", [128, 1024], BF16),
            ("xT", [128, 8, 128], BF16),
            ("bias", [128, 1024], F32),
        ])
        P = psum(tag, [
            ("t", [128, 8, 128], BF16),
            ("w", [128, 8, 128], F32),
            ("y", [128, 1024], F32),
            ("m", [128, 128], F32),
        ])
        dW = p.dma("sync", A["wstage"][:, :, :],
                   WD["L1_ffn_weight"].ap().rearrange("(r p) c -> p r c", p=128), inc="sD")
        tAm = E("vector", lambda v: v.tensor_reduce(
            scal[:, 0:1], A["wstage"].rearrange("p a b -> p (a b)"), axis=AX,
            op=OP.max, apply_absolute_value=True), waits=[dW], inc="sV")
        tWm = cross_part_max(scal[:, 0:1], arow[0:1, 0:1], P["m"], [tAm])
        tWm2 = E("vector", lambda v: v.tensor_single_scalar(arow[0:1, 0:1], arow[0:1, 0:1],
                                                            1e-12, OP.max), waits=[tWm], inc="sV")
        tWb = bcast_scalar(scal[:, 1:2], arow[0:1, 0:1], P["m"][:, 0:1], [tWm2])
        tWr = E("vector", lambda v: v.reciprocal(scal[:, 2:3], scal[:, 1:2]), waits=[tWb],
                inc="sV")
        tWs = E("vector", lambda v: v.tensor_scalar(scal[:, 2:3], scal[:, 2:3], 448.0, None,
                                                    OP.mult), waits=[tWr], inc="sV")
        wq_tok = None
        for j in range(8):
            trs = None
            for r in range(8):
                trs = E("tensor", lambda t, j=j, r=r: t.transpose(
                    P["w"][:, r, :], A["wstage"][:, r, j * 128:(j + 1) * 128], ident_f[:, :]),
                    waits=T(tIf, dW, wq_tok), inc="sT")
            wq_tok = quant_chain(P["w"].rearrange("p a b -> p (a b)"), A["wq"][:, j, :],
                                 scal[:, 2:3], A["q1"], A["q2"], A["q3"], A["msk"],
                                 T(trs, tWs))
        dB = p.dma("sync", A["q1"][0:1, :], WD["L1_ffn_bias"].ap().rearrange("(a c) -> a c", a=1),
                   waits=[wq_tok], inc="sD")
        tb1 = E("tensor", lambda t: t.matmul(P["y"][:, 0:512], ones_row[:, :],
                                             A["q1"][0:1, 0:512], start=True, stop=True),
                waits=T(tOnes, dB), inc=None)
        tb2 = E("tensor", lambda t: t.matmul(P["y"][:, 512:1024], ones_row[:, :],
                                             A["q1"][0:1, 512:1024], start=True, stop=True),
                inc="sT", fence=True)
        tb3 = E("vector", lambda v: v.tensor_scalar(A["bias"][:, :], P["y"][:, :], 0.5, None,
                                                    OP.mult), waits=[tb2], inc="sV")
        xg_free = [None, None]
        am_tok = None
        for g in range(NG):
            xg = A["xg"][:, g % 2]
            dX = p.dma("sync", xg, x_re(src)[g], waits=T(xg_free[g % 2], tb3, wq_tok), inc="sD")
            for t in range(TG):
                t8 = ln_tile(xg[:, t, :], A["xf"], 8 * ((TG * g + t) % 8), T(dX, am_tok))
                tr = E("vector", lambda v: v.tensor_reduce(scal[:, 4:5], A["xf"][:, :], axis=AX,
                                                           op=OP.max, apply_absolute_value=True),
                       waits=[t8], inc="sV")
                if am_tok is None:
                    am_tok = E("vector", lambda v: v.tensor_copy(scal[:, 5:6], scal[:, 4:5]),
                               waits=[tr], inc="sV")
                else:
                    am_tok = E("vector", lambda v: v.tensor_tensor(scal[:, 5:6], scal[:, 5:6],
                                                                   scal[:, 4:5], OP.max),
                               waits=[tr], inc="sV")
            xg_free[g % 2] = am_tok
        tXm = cross_part_max(scal[:, 5:6], arow[0:1, 1:2], P["m"], [am_tok])
        tXm2 = E("vector", lambda v: v.tensor_single_scalar(arow[0:1, 1:2], arow[0:1, 1:2],
                                                            1e-12, OP.max), waits=[tXm], inc="sV")
        dC1 = p.dma("gpsimd", cci[:, :], arow[0:1, 1:2], waits=[tXm2], inc="sDg")
        tCC = E("gpsimd", lambda g_: g_.collective_compute(
            "AllReduce", OP.max, replica_groups=[list(range(8))],
            ins=[cci.ap()], outs=[cco.ap()]), waits=[dC1], inc="sC")
        dC2 = p.dma("scalar", arow[0:1, 2:3], cco[:, :], waits=[tCC], inc="sD")
        tXb = bcast_scalar(scal[:, 6:7], arow[0:1, 2:3], P["m"][:, 0:1], [dC2])
        tXr = E("vector", lambda v: v.reciprocal(scal[:, 7:8], scal[:, 6:7]), waits=[tXb],
                inc="sV")
        tXs = E("vector", lambda v: v.tensor_scalar(scal[:, 7:8], scal[:, 7:8], 448.0, None,
                                                    OP.mult), waits=[tXr], inc="sV")
        tE1 = E("vector", lambda v: v.tensor_tensor(scal[:, 8:9], scal[:, 6:7], scal[:, 1:2],
                                                    OP.mult), waits=[tXs, tWb], inc="sV")
        tE2 = E("vector", lambda v: v.tensor_scalar(scal[:, 8:9], scal[:, 8:9],
                                                    float(0.5 / (448.0 * 448.0)), None, OP.mult),
                waits=[tE1], inc="sV")
        dO = None
        mm_prev = None
        tq = None
        to1 = None
        te = None
        for g in range(NG):
            xg = A["xg"][:, g % 2]
            dX = p.dma("sync", xg, x_re(src)[g], waits=T(xg_free[g % 2], tE2), inc="sD")
            out_toks = []
            for t in range(TG):
                t8 = ln_tile(xg[:, t, :], A["xf"], 8 * ((TG * g + t) % 8), T(dX, tq))
                tq = quant_chain(A["xf"][:, :], A["xb"][:, :], scal[:, 7:8],
                                 A["q1"], A["q2"], A["q3"], A["msk"],
                                 T(t8, tXs, wq_tok, tb3, te))
                te = transposes_to(P["t"], A["xb"], A["xT"][:, :, :], [tq], extra=T(mm_prev))
                mm = None
                for cc in range(8):
                    for h in range(2):
                        last = (cc == 7 and h == 1)
                        mm = E("tensor", lambda tt, cc=cc, h=h: tt.matmul(
                            P["y"][:, h * 512:(h + 1) * 512], A["xT"][:, cc, :],
                            A["wq"][:, cc, h * 512:(h + 1) * 512],
                            start=(cc == 0), stop=(cc == 7)), waits=T(te, to1, tb3),
                            inc="sT" if last else None, fence=last)
                mm_prev = mm
                to1 = E("vector", lambda v, t=t, xg=xg: v.scalar_tensor_tensor(
                    xg[:, t, :], P["y"][:, :], scal[:, 8:9], xg[:, t, :], OP.mult, OP.add),
                    waits=T(mm, tE2, tb3), inc="sV")
                to2 = E("vector", lambda v, t=t, xg=xg: v.tensor_tensor(
                    xg[:, t, :], xg[:, t, :], A["bias"][:, :], OP.add),
                    waits=[to1], inc="sV")
                out_toks.append(to2)
            dO = p.dma("scalar", x_re(dst)[g], xg, waits=out_toks, inc="sDo")
            xg_free[g % 2] = dO
        return dO

    def phase_l1attn(tag, src, dst):
        A = arena(tag, [
            ("wqkv", [128, 8, 24, 128], BF16),
            ("wstage", [128, 1024], F32),
            ("wstage2", [128, 8, 3, 2, 64], F32),
            ("xg", [128, 2, TG, 1024], F32),
            ("x0", [128, 1024], F32),
            ("xb", [128, 1024], BF16),
            ("xT", [128, 8, 512], BF16),
            ("st", [128, 6, 512], BF16),
            ("vst2", [128, 2, 128], BF16),
            ("ktb", [128, 2, S], BF16),
            ("vtb", [128, 2, S // 128, 128], BF16),
            ("qtg", [128, 2, 8, 512], BF16),
            ("PT", [128, 2, 1024], BF16),
            ("oT", [128, 8, 512], BF16),
            ("rr", [128, 512], F32),
            ("wproj", [128, 8, 1024], BF16),
            ("bpr", [128, 1024], F32),
        ])
        P1 = psum(tag + "1", [
            ("t", [128, 8, 128], BF16),
            ("s2", [128, 2, 512], F32),
        ])
        psTv = P1["t"]
        P = psum(tag + "2", [
            ("s", [128, 2, 1024], F32),
            ("o", [128, 512], F32),
            ("n", [128, 512], F32),
            ("y", [128, 1024], F32),
        ])
        wq_toks = []
        dprev = None
        for cc in range(8):
            dW = None
            for k in range(3):
                for i in range(2):
                    dW = p.dma("sync", A["wstage2"][:, :, k, i, :],
                               bass.AP(WD["L1_attn_W_qkv"],
                                       cc * 128 * 3072 + 64 * k + 192 * i,
                                       [[3072, 128], [384, 8], [1, 64]]),
                               waits=T(dprev), inc="sD")
            dprev = E("vector", lambda v, cc=cc: v.tensor_copy(
                A["wqkv"][:, cc, :, :].rearrange("p a b -> p (a b)"),
                A["wstage2"].rearrange("p a b c d -> p (a b c d)")), waits=[dW], inc="sV")
            wq_toks.append(dprev)
        bq_toks = []
        for pr in range(8):
            for k in range(3):
                bq_toks.append(p.dma("sync", bqkv_col[:, pr * 3 + k:pr * 3 + k + 1],
                                     bass.AP(WD["L1_attn_b_qkv"], 384 * pr + 64 * k,
                                             [[192, 2], [1, 64]]), inc="sD"))
        WQ = A["wqkv"]
        xg_free = [None, None]
        st_free = [None] * 6
        ev_last = [None, None]
        vst2_free = [None, None]
        tr_prev = None
        qkv_dma = []
        for g in range(NG):
            xg = A["xg"][:, g % 2]
            dX = p.dma("sync", xg, x_re(src)[g], waits=T(xg_free[g % 2], *wq_toks[-2:]), inc="sD")
            tr_evac = []
            t9 = None
            te = None
            for t in range(TG):
                t9 = ln_tile(xg[:, t, :], A["x0"], 8 * ((TG * g + t) % 8), T(dX, t9, te),
                             bf_out=A["xb"])
                te = transposes_to(psTv, A["xb"], A["xT"][:, :, 128 * t:128 * (t + 1)],
                                   [t9], extra=T(tr_evac[-1] if tr_evac else None, tr_prev,
                                                 *wq_toks[-1:]))
                tr_evac.append(te)
            tr_prev = tr_evac[-1]
            for pr in range(8):
                for k in range(3):
                    b_ = (pr * 3 + k) % 2
                    mm = None
                    for cc in range(8):
                        mm = E("tensor", lambda t, pr=pr, k=k, cc=cc, b_=b_: t.matmul(
                            P1["s2"][:, b_, :], WQ[:, cc, pr * 3 + k, :],
                            A["xT"][:, cc, :], start=(cc == 0), stop=(cc == 7)),
                            waits=T(tr_evac[-1], wq_toks[-1], ev_last[b_]),
                            inc="sT" if cc == 7 else None, fence=(cc == 7))
                    bci = pr * 3 + k
                    si = k * 2 + (pr % 2)
                    ev = E("vector", lambda v, b_=b_, bci=bci, si=si: v.tensor_scalar(
                        A["st"][:, si, :], P1["s2"][:, b_, :], bqkv_col[:, bci:bci + 1],
                        None, OP.add), waits=T(mm, bq_toks[-1], st_free[si]), inc="sV")
                    ev_last[b_] = ev
                    if k == 0:
                        d_ = p.dma("scalar", bass.AP(qTd, pr * 128 * S + 512 * g,
                                                   [[S, 128], [1, 512]]),
                                   A["st"][:, si, :], waits=[ev], inc="sDo")
                        st_free[si] = d_
                    elif k == 1:
                        d_ = p.dma("scalar", bass.AP(ktd, pr * 128 * S + 512 * g,
                                                   [[S, 128], [1, 512]]),
                                   A["st"][:, si, :], waits=[ev], inc="sDo")
                        st_free[si] = d_
                    else:
                        tt = None
                        tv = None
                        for sx in range(TG):
                            tt = E("tensor", lambda t, sx=sx, si=si: t.transpose(
                                psTv[:, sx, :], A["st"][:, si, 128 * sx:128 * (sx + 1)],
                                ident_b[:, :]), waits=T(ev, tIb, tr_prev, tv), inc="sT",
                                fence=True)
                            tv = E("vector", lambda v, sx=sx: v.tensor_copy(
                                A["vst2"][:, sx % 2, :],
                                psTv[:, sx, :]), waits=T(tt, vst2_free[sx % 2]), inc="sV")
                            d_ = p.dma("scalar",
                                       bass.AP(vtd, pr * 128 * S + (4 * g + sx) * 128,
                                               [[S, 128], [1, 128]]),
                                       A["vst2"][:, sx % 2, :], waits=[tv], inc="sDo")
                            vst2_free[sx % 2] = d_
                        st_free[si] = tt
                        tr_prev = vst2_free[(TG - 1) % 2]
                    qkv_dma.append(st_free[si])
            xg_free[g % 2] = tr_evac[-1]
        d1_dma_done = ("sDo", p.counts["sDo"])
        wp_tok = None
        for j in range(8):
            dW = p.dma("sync", A["wstage"][:, :],
                       bass.AP(WD["L1_attn_W_proj"], j * 128 * 1024, [[1024, 128], [1, 1024]]),
                       waits=T(wp_tok), inc="sD")
            wp_tok = E("vector", lambda v, j=j: v.tensor_copy(A["wproj"][:, j, :],
                                                              A["wstage"][:, :]),
                       waits=[dW], inc="sV")
        dB = p.dma("sync", A["wstage"][0:1, :],
                   WD["L1_attn_b_proj"].ap().rearrange("(a c) -> a c", a=1), inc="sD")
        tb1 = E("tensor", lambda t: t.matmul(P["y"][:, 0:512], ones_row[:, :],
                                             A["wstage"][0:1, 0:512], start=True, stop=True),
                waits=T(tOnes, dB, tr_prev), inc=None)
        tb2 = E("tensor", lambda t: t.matmul(P["y"][:, 512:1024], ones_row[:, :],
                                             A["wstage"][0:1, 512:1024], start=True, stop=True),
                inc="sT", fence=True)
        tb3 = E("vector", lambda v: v.tensor_copy(A["bpr"][:, :], P["y"][:, :]),
                waits=[tb2], inc="sV")
        dO = None
        xg_free2 = None
        qtg_free = [None, None]
        ktb_free = [None, None]
        oT_free = None
        y_mm_prev = None
        exp_last = [None, None]
        pt_last = [None, None]
        y_mm_last_qg = None
        to1_qg_prev = None
        for qg in range(NG):
            qb = qg % 2
            dQ = p.dma("sync", A["qtg"][:, qb, :, :],
                       bass.AP(qTd, 512 * qg, [[S, 128], [128 * S, 8], [1, 512]]),
                       waits=T(qtg_free[qb], d1_dma_done), inc="sD")
            dX = p.dma("sync", A["xg"][:, 0], x_re(src)[qg], waits=T(xg_free2, d1_dma_done),
                       inc="sD")
            pair_out = []
            for pr in range(8):
                kb = pr % 2
                dK = p.dma("sync", A["ktb"][:, kb, :],
                           bass.AP(ktd, pr * 128 * S, [[S, 128], [1, S]]),
                           waits=T(ktb_free[kb], d1_dma_done), inc="sD")
                dV = p.dma("sync", A["vtb"][:, kb, :, :].rearrange("p a b -> p (a b)"),
                           bass.AP(vtd, pr * 128 * S, [[S, 128], [1, S]]),
                           waits=T(ktb_free[kb]), inc="sD")
                NC_ = S // 128
                expD = [None] * NC_
                pv_toks = [None] * NC_
                for sc in range(NC_ + 1):
                    if sc < NC_:
                        par = sc % 2
                        E("tensor", lambda t, sc=sc, par=par, qb=qb, pr=pr, kb=kb: t.matmul(
                            P["s"][:, par, 0:512], A["ktb"][0:64, kb, 128 * sc:128 * (sc + 1)],
                            A["qtg"][0:64, qb, pr, :], start=True, stop=True,
                            tile_position=(0, 0)),
                            waits=T(dK, dQ, dV, expD[sc - 2] if sc >= 2 else None,
                                    y_mm_prev), inc=None)
                        m2 = E("tensor", lambda t, sc=sc, par=par, qb=qb, pr=pr, kb=kb: t.matmul(
                            P["s"][:, par, 512:1024],
                            A["ktb"][64:128, kb, 128 * sc:128 * (sc + 1)],
                            A["qtg"][64:128, qb, pr, :], start=True, stop=True,
                            tile_position=(64, 0), skip_group_check=True),
                            inc="sT", fence=True)
                        expD[sc] = E("scalar", lambda s, sc=sc, par=par: s.activation(
                            A["PT"][:, sc % 2, :], P["s"][:, par, :], AF.Exp),
                            waits=T(m2, pv_toks[sc - 2] if sc >= 2 else None), inc="sS",
                            fence=True)
                    if sc >= 1:
                        sp = sc - 1
                        lastc = (sp == NC_ - 1)
                        E("tensor", lambda t, sp=sp, pr=pr, kb=kb: t.matmul(
                            P["o"][0:64, :], A["vtb"][:, kb, sp, 0:64],
                            A["PT"][:, sp % 2, 0:512], start=(sp == 0),
                            stop=(sp == NC_ - 1), tile_position=(0, 0)),
                            waits=T(expD[sp], dV, oT_free), inc=None)
                        E("tensor", lambda t, sp=sp, pr=pr, kb=kb: t.matmul(
                            P["o"][64:128, :], A["vtb"][:, kb, sp, 64:128],
                            A["PT"][:, sp % 2, 512:1024], start=(sp == 0),
                            stop=(sp == NC_ - 1), tile_position=(0, 64),
                            skip_group_check=True), inc=None)
                        E("tensor", lambda t, sp=sp: t.matmul(
                            P["n"][0:64, :], ones64[:, :], A["PT"][:, sp % 2, 0:512],
                            start=(sp == 0), stop=(sp == NC_ - 1), tile_position=(0, 0),
                            skip_group_check=True),
                            waits=T(tO64), inc=None)
                        pv_toks[sp] = E("tensor", lambda t, sp=sp: t.matmul(
                            P["n"][64:128, :], ones64[:, :], A["PT"][:, sp % 2, 512:1024],
                            start=(sp == 0), stop=(sp == NC_ - 1), tile_position=(0, 64),
                            skip_group_check=True), inc="sT", fence=lastc)
                ktb_free[kb] = pv_toks[NC_ - 1]
                trr = E("vector", lambda v: v.reciprocal(A["rr"][:, :], P["n"][:, :]),
                        waits=[pv_toks[NC_ - 1]], inc="sV")
                tev = E("vector", lambda v, pr=pr: v.tensor_tensor(
                    A["oT"][:, pr, :], P["o"][:, :], A["rr"][:, :], OP.mult),
                    waits=T(trr, y_mm_last_qg), inc="sV")
                oT_free = tev
                pair_out.append(tev)
            qtg_free[qb] = pair_out[-1]
            out_toks = []
            to1 = None
            for t in range(TG):
                mm = None
                for pr in range(8):
                    for h in range(2):
                        last = (pr == 7 and h == 1)
                        mm = E("tensor", lambda tt, t=t, pr=pr, h=h: tt.matmul(
                            P["y"][:, h * 512:(h + 1) * 512],
                            A["oT"][:, pr, 128 * t:128 * (t + 1)],
                            A["wproj"][:, pr, h * 512:(h + 1) * 512],
                            start=(pr == 0), stop=(pr == 7)),
                            waits=T(pair_out[-1], wp_tok, to1, to1_qg_prev, tb3),
                            inc="sT" if last else None, fence=last)
                y_mm_prev = mm
                y_mm_last_qg = mm
                to1 = E("vector", lambda v, t=t: v.tensor_tensor(
                    A["xg"][:, 0, t, :], A["xg"][:, 0, t, :], P["y"][:, :], OP.add),
                    waits=T(mm, dX, tb3), inc="sV")
                to2 = E("vector", lambda v, t=t: v.tensor_tensor(
                    A["xg"][:, 0, t, :], A["xg"][:, 0, t, :], A["bpr"][:, :], OP.add),
                    waits=[to1], inc="sV")
                out_toks.append(to2)
            to1_qg_prev = to1
            dO = p.dma("scalar", x_re(dst)[qg], A["xg"][:, 0], waits=out_toks, inc="sDo")
            xg_free2 = dO
        return dO

    chain_src = [x_in, xs[0], xs[1], xs[2], xs[3], xs[4]]
    chain_dst = [xs[0], xs[1], xs[2], xs[3], xs[4], y_out]
    if n_sub < 6:
        chain_dst[n_sub - 1] = y_out
    phases = [
        lambda i: phase_l0ffn("A", chain_src[0], chain_dst[0]),
        lambda i: phase_l0attn("B", chain_src[1], chain_dst[1]),
        lambda i: phase_l0ffn("A2", chain_src[2], chain_dst[2]),
        lambda i: phase_l1ffn("CC", chain_src[3], chain_dst[3], *ccio[0]),
        lambda i: phase_l1attn("D", chain_src[4], chain_dst[4]),
        lambda i: phase_l1ffn("EE", chain_src[5], chain_dst[5], *ccio[1]),
    ]
    for i in range(n_sub):
        phases[i](i)
        p.barrier()
    E("sync", lambda e: e.nop(), waits=[("sDo", p.counts["sDo"]), ("sDg", p.counts["sDg"])])

    sems = {n_: nc.alloc_semaphore(n_) for n_ in
            ("sD", "sDo", "sDg", "sV", "sS", "sT", "sG", "sC")}
    with nc.Block() as block:
        p.build_into(block, sems)
    return nc


def kernel(**inputs):
    x_all = np.ascontiguousarray(inputs["L0_attn_x"], dtype=np.float32)
    weights = {k: np.ascontiguousarray(v, dtype=np.float32) for k, v in inputs.items()
               if k != "L0_attn_x"}
    nc = build_program()
    in_maps = [dict(x=x_all[i], **weights) for i in range(8)]
    res = run_bass_kernel_spmd(nc, in_maps, core_ids=list(range(8)))
    return np.stack([np.asarray(res.results[i]["y"], dtype=np.float32) for i in range(8)],
                    axis=0)
